# version 03
# baseline (speedup 1.0000x reference)
import contextlib
import numpy as np
import ml_dtypes

import concourse.bass as bass
import concourse.mybir as mybir
from concourse.bass_utils import run_bass_kernel_spmd

F32 = mybir.dt.float32
BF16 = mybir.dt.bfloat16
F16 = mybir.dt.float16
U8 = mybir.dt.uint8
AF = mybir.ActivationFunctionType
ALU = mybir.AluOpType
AX = mybir.AxisListType
BF = ml_dtypes.bfloat16

D = 1024
T = 2048
TC = 512
NTC = T // TC
DI = 2048
NST = 16
DTR = 64

ENGINES = ("sync", "tensor", "vector", "scalar", "gpsimd")


class Seq:

    def __init__(self, nc, strict=False):
        self.nc = nc
        self.strict = strict
        self.ops = []
        self.access = {}
        self.floor = {}
        self._last_barrier = None
        self.MAXACC = 48

    @staticmethod
    def _interval(ap):
        off = ap.offset
        span = 0
        for step, count in ap.ap:
            span += abs(step) * (count - 1)
        lo = off if all(s >= 0 for s, _ in ap.ap) else off - span
        return ap.tensor.name, lo, lo + span

    @staticmethod
    def _is_psum(ap):
        return type(ap.tensor).__name__ == "PSumTensorHandle"

    def _deps_for(self, ins, outs):
        deps = set()

        def scan(name, lo, hi, writing):
            fl = self.floor.get(name)
            if fl is not None:
                deps.add(fl)
            for idx, alo, ahi, wr in self.access.get(name, ()):
                if ahi < lo or alo > hi:
                    continue
                if wr or writing:
                    deps.add(idx)

        for ap in ins:
            name, lo, hi = self._interval(ap)
            scan(name, lo, hi, self._is_psum(ap))
        for ap in outs:
            name, lo, hi = self._interval(ap)
            scan(name, lo, hi, True)
        return deps

    def op(self, engine, emit_fn, ins=(), outs=(), dma=False):
        assert engine in ENGINES, engine
        i = len(self.ops)
        if self.strict:
            deps = {i - 1} if i else set()
        else:
            deps = self._deps_for(ins, outs)
            if self._last_barrier is not None:
                deps.add(self._last_barrier)
        self.ops.append((engine, emit_fn, dma, deps))
        for ap in ins:
            name, lo, hi = self._interval(ap)
            self.access.setdefault(name, []).append((i, lo, hi, self._is_psum(ap)))
        for ap in outs:
            name, lo, hi = self._interval(ap)
            self.access.setdefault(name, []).append((i, lo, hi, True))
        for name, lst in list(self.access.items()):
            if len(lst) > self.MAXACC:
                drop = lst[: len(lst) - self.MAXACC]
                self.floor[name] = max(
                    max(d[0] for d in drop), self.floor.get(name, -1)
                )
                self.access[name] = lst[len(lst) - self.MAXACC:]
        return i

    def barrier(self):
        i = len(self.ops) - 1
        if i < 0:
            return
        for name in list(self.access):
            self.floor[name] = i
            self.access[name] = []
        self._last_barrier = i

    def emit(self, block, sems):
        nops = len(self.ops)
        inc = {e: [0] * (nops + 1) for e in ENGINES}
        for i, (e, _f, dma, _d) in enumerate(self.ops):
            for e2 in ENGINES:
                inc[e2][i + 1] = inc[e2][i] + ((16 if dma else 1) if e2 == e else 0)
        gf = getattr(self, "_global_floor", None)

        def make_body(eng_name):
            def body(eng):
                last_waited = dict.fromkeys(ENGINES, 0)
                for i, (e, emit_fn, dma, deps) in enumerate(self.ops):
                    if e != eng_name:
                        continue
                    need = dict.fromkeys(ENGINES, 0)
                    dd = set(deps)
                    if gf is not None and i > gf:
                        dd.add(gf)
                    for d in dd:
                        for e2 in ENGINES:
                            v = inc[e2][d + 1]
                            if v > need[e2]:
                                need[e2] = v
                    for e2 in ENGINES:
                        if e2 == eng_name and e2 == "tensor":
                            continue
                        if need[e2] > last_waited[e2]:
                            eng.wait_ge(sems[e2], need[e2])
                            last_waited[e2] = need[e2]
                    inst = emit_fn(eng)
                    assert inst is not None, f"op {i} returned None"
                    inst.then_inc(sems[e], 16 if dma else 1)
                if eng_name == "sync":
                    for e2 in ENGINES:
                        tot = inc[e2][nops]
                        if tot > last_waited[e2]:
                            eng.wait_ge(sems[e2], tot)
            return body

        block.sync(make_body("sync"))
        block.tensor(make_body("tensor"))
        block.vector(make_body("vector"))
        block.scalar(make_body("scalar"))
        block.gpsimd(make_body("gpsimd"))


class Prog:

    def __init__(self, name, strict=False):
        self.nc = bass.Bass("TRN2", target_bir_lowering=False, name=name)
        self.seq = Seq(self.nc, strict=strict)
        self.inputs = {}
        self.outputs = {}
        self._const_cols = {}
        self._setup_done = False
        self._sb_cache = {}
        self._persist = set()
        self._arena = None
        self._bump = 0
        self._stage = "G"

    def din(self, name, shape, dt=F32):
        h = self.nc.dram_tensor(name, list(shape), dt, kind="ExternalInput")
        self.inputs[name] = h
        return h.ap()

    def dout(self, name, shape, dt=F32):
        h = self.nc.dram_tensor(name, list(shape), dt, kind="ExternalOutput")
        self.outputs[name] = h
        return h.ap()

    def dscratch(self, name, shape, dt=F32):
        return self.nc.dram_tensor(name, list(shape), dt, kind="Internal").ap()

    def make_arena(self, reserve=6 * 1024):
        nbytes = self.nc.sbuf_bytes_remaining - reserve
        h = self.nc.alloc_sbuf_tensor("stage_arena", [128, nbytes], U8)
        base = self.nc.lookup_mloc(h).addr
        self._arena = (base, nbytes)

    def begin_stage(self, tag):
        self.seq.barrier()
        self._stage = tag
        self._bump = 0
        self._sb_cache = {k: v for k, v in self._sb_cache.items()
                          if k in self._persist}

    def sb(self, name, shape, dt=F32, persist=False):
        if name in self._sb_cache:
            t, sh, d = self._sb_cache[name]
            assert list(sh) == list(shape) and d == dt, f"sb {name} shape clash"
            return t
        if persist or self._arena is None:
            t = self.nc.alloc_sbuf_tensor(name, list(shape), dt)
            if persist:
                self._persist.add(name)
        else:
            per_part = int(np.prod(shape[1:])) * mybir.dt.size(dt)
            off = (self._bump + 31) & ~31
            base, nbytes = self._arena
            assert off + per_part <= nbytes, \
                f"arena overflow: {self._stage}/{name} needs {per_part} at {off} (cap {nbytes})"
            t = self.nc.alloc_sbuf_tensor_at(
                f"{self._stage}_{name}", list(shape), dt, offset=base + off)
            self._bump = off + per_part
        self._sb_cache[name] = (t, list(shape), dt)
        return t

    def ps(self, name, shape, dt=F32):
        return self.nc.alloc_psum_tensor(name, list(shape), dt)

    def sb_alias(self, name, existing, shape, dt):
        t, sh, d = self._sb_cache[existing]
        assert list(sh) == list(shape) and d == dt
        self._sb_cache[name] = (t, sh, d)

    def inline(self, arr, name):
        if not hasattr(self, "_inline_cache"):
            self._inline_cache = {}
        if name not in self._inline_cache:
            self._inline_cache[name] = self.nc.inline_tensor(
                np.ascontiguousarray(arr), name)
        return self._inline_cache[name]

    def setup_consts(self):
        assert not self._setup_done
        self._setup_done = True
        self.ones_r_f32 = self.sb("ones_r_f32", [1, 128], F32, persist=True)
        self.ones_r_bf = self.sb("ones_r_bf", [1, 128], BF16, persist=True)
        self.ones_c_f32 = self.sb("ones_c_f32", [128, 1], F32, persist=True)
        self.ones_c_bf = self.sb("ones_c_bf", [128, 1], BF16, persist=True)
        self.ones_r_f16 = self.sb("ones_r_f16", [1, 128], F16, persist=True)
        self.ones_c_f16 = self.sb("ones_c_f16", [128, 1], F16, persist=True)
        for t in (self.ones_r_f32, self.ones_r_bf, self.ones_c_f32,
                  self.ones_c_bf, self.ones_r_f16, self.ones_c_f16):
            self.v(lambda e, t=t: e.memset(t[:], 1.0), outs=[t[:]])
        self.pp = [self.ps(f"pp{i}", [128, 512], F32) for i in range(4)]
        self.pr = [self.ps(f"pr{i}", [1, 512], F32) for i in range(2)]
        self.pq = self.ps("pq", [2, TC], F32)

    def const_col(self, value):
        if value not in self._const_cols:
            t = self.sb(f"cc_{len(self._const_cols)}", [128, 1], F32,
                        persist=True)
            self.v(lambda e: e.memset(t[:], float(value)), outs=[t[:]])
            self._const_cols[value] = t
        return self._const_cols[value][:]

    def dma(self, out, in_, engine="sync"):
        self.seq.op(engine, lambda e: e.dma_start(out=out, in_=in_),
                    ins=[in_], outs=[out], dma=True)

    def mm(self, out, lhsT, rhs, start=True, stop=True):
        self.seq.op("tensor",
                    lambda e: e.matmul(out, lhsT, rhs, start=start, stop=stop),
                    ins=[lhsT, rhs], outs=[out])

    def trans(self, out, in_, identity):
        self.seq.op("tensor", lambda e: e.transpose(out, in_, identity),
                    ins=[in_, identity], outs=[out])

    def act(self, out, in_, func, bias=0.0, scale=1.0, accum_out=None):
        ins = [in_]
        if not isinstance(bias, float):
            ins.append(bias)
        if not isinstance(scale, float):
            ins.append(scale)
        outs = [out] + ([accum_out] if accum_out is not None else [])
        if accum_out is not None:
            self.seq.op("scalar", lambda e: e.activation(
                out, in_, func, bias=bias, scale=scale, accum_out=accum_out),
                ins=ins, outs=outs)
        else:
            self.seq.op("scalar", lambda e: e.activation(
                out, in_, func, bias=bias, scale=scale), ins=ins, outs=outs)

    def acopy(self, out, in_):
        self.seq.op("scalar", lambda e: e.copy(out, in_), ins=[in_], outs=[out])

    def v(self, emit, ins=(), outs=()):
        self.seq.op("vector", emit, ins=ins, outs=outs)

    def g(self, emit, ins=(), outs=()):
        self.seq.op("gpsimd", emit, ins=ins, outs=outs)

    def tt(self, out, a, b, op, engine="vector"):
        self.seq.op(engine, lambda e: e.tensor_tensor(out, a, b, op),
                    ins=[a, b], outs=[out])

    def ts(self, out, in0, s1, s2, op0, op1=None, engine="vector"):
        ins = [in0] + [x for x in (s1, s2) if x is not None and not isinstance(x, float)]
        if op1 is None:
            self.seq.op(engine, lambda e: e.tensor_scalar(out, in0, s1, s2, op0),
                        ins=ins, outs=[out])
        else:
            self.seq.op(engine, lambda e: e.tensor_scalar(out, in0, s1, s2, op0, op1),
                        ins=ins, outs=[out])

    def stt(self, out, in0, scalar, in1, op0, op1, engine="vector"):
        ins = [in0, in1] + ([scalar] if not isinstance(scalar, float) else [])
        self.seq.op(engine, lambda e: e.scalar_tensor_tensor(
            out, in0, scalar, in1, op0, op1), ins=ins, outs=[out])

    def tts(self, out, d0, d1, init, op0, op1):
        ins = [d0, d1] + ([init] if not isinstance(init, float) else [])
        self.seq.op("vector", lambda e: e.tensor_tensor_scan(
            out, d0, d1, init, op0, op1), ins=ins, outs=[out])

    def vcopy(self, out, in_):
        self.v(lambda e: e.tensor_copy(out, in_), ins=[in_], outs=[out])

    def reduce(self, out, in_, op, negate=False):
        self.seq.op("vector", lambda e: e.tensor_reduce(
            out, in_, AX.X, op, negate=negate), ins=[in_], outs=[out])

    def recip(self, out, in_):
        self.seq.op("vector", lambda e: e.reciprocal(out, in_),
                    ins=[in_], outs=[out])

    def memset(self, ap, val, engine="vector"):
        self.seq.op(engine, lambda e: e.memset(ap, val), ins=[], outs=[ap])

    def run(self, in_maps, trace=False):
        with contextlib.ExitStack() as st:
            sems = {e: st.enter_context(self.nc.semaphore(f"s_{e}"))
                    for e in ENGINES}
            blk = st.enter_context(self.nc.Block())
            self.seq.emit(blk, sems)
        return run_bass_kernel_spmd(self.nc, in_maps,
                                    core_ids=list(range(len(in_maps))),
                                    trace=trace)


def rearr3(sb_handle, c, inner):
    return sb_handle[:].rearrange("p (c m) -> p c m", c=c)


def load_w_cast(p, name, hbm_ap, kchunks, mcols, dt=BF16):
    t = p.sb(name, [128, kchunks * mcols], dt)
    src = hbm_ap.rearrange("(c p) m -> p c m", p=128)
    dst = rearr3(t, kchunks, mcols)
    p.dma(dst, src, engine="gpsimd")
    return t


def emit_ln(p, x_chunks_f32, out_chunks_bf, tc_len, pref, g_ap=None, b_ap=None,
            out_chunks_f32=None, nchunks=8, dm=None, dt16=BF16):
    dm = dm or (128 * nchunks)
    assert tc_len <= 512
    sums = p.pr[0][0:1, 0:tc_len]
    sqs = p.pr[1][0:1, 0:tc_len]
    xb = p.sb(f"lnsc_xb{nchunks}", [128, nchunks * tc_len], dt16)
    xq = p.sb(f"lnsc_xq{nchunks}", [128, nchunks * tc_len], dt16)
    ones_c16 = p.ones_c_bf if dt16 == BF16 else p.ones_c_f16
    for c in range(nchunks):
        xbc = xb[:, c * tc_len:(c + 1) * tc_len]
        p.vcopy(xbc, x_chunks_f32[c])
        p.act(xq[:, c * tc_len:(c + 1) * tc_len], x_chunks_f32[c], AF.Square)
    for c in range(nchunks):
        p.mm(sums, ones_c16[:], xb[:, c * tc_len:(c + 1) * tc_len],
             start=(c == 0), stop=(c == nchunks - 1))
    for c in range(nchunks):
        p.mm(sqs, ones_c16[:], xq[:, c * tc_len:(c + 1) * tc_len],
             start=(c == 0), stop=(c == nchunks - 1))
    mu = p.sb("lnsc_mu", [1, 512], F32)[0:1, 0:tc_len]
    rstd = p.sb("lnsc_rs", [1, 512], F32)[0:1, 0:tc_len]
    var = p.sb("lnsc_va", [1, 512], F32)[0:1, 0:tc_len]
    mu2 = p.sb("lnsc_m2", [1, 512], F32)[0:1, 0:tc_len]
    p.ts(mu, sums, 1.0 / dm, None, ALU.mult)
    p.ts(var, sqs, 1.0 / dm, None, ALU.mult)
    p.act(mu2, mu, AF.Square)
    p.tt(var, var, mu2, ALU.subtract)
    p.act(var, var, AF.Sqrt, bias=p.const_col(1e-5)[0:1, :])
    p.recip(rstd, var)
    mub = p.pp[2][:, 0:tc_len]
    rsb = p.pp[3][:, 0:tc_len]
    p.mm(mub, p.ones_r_f32[:], mu)
    p.mm(rsb, p.ones_r_f32[:], rstd)
    tmp = p.sb("lnsc_t", [128, 512], F32)[:, 0:tc_len]
    for c in range(nchunks):
        p.tt(tmp, x_chunks_f32[c], mub, ALU.subtract)
        dst_f32 = out_chunks_f32[c] if out_chunks_f32 is not None else tmp
        p.tt(dst_f32, tmp, rsb, ALU.mult)
        if g_ap is not None:
            p.ts(dst_f32, dst_f32, g_ap[c], b_ap[c], ALU.mult, ALU.add)
        if out_chunks_bf is not None:
            p.vcopy(out_chunks_bf[c], dst_f32)


def declare_A_io(p, ext=True):
    io = {
        "xT": p.din("xT", [D, T], F32),
        "w_inproj": p.din("w_inproj", [D, 2 * DI], F32),
        "w_xproj": p.din("w_xproj", [DI, 96], F32),
        "w_dtproj": p.din("w_dtproj", [DTR, DI], F32),
        "convw_pack": p.din("convw_pack", [128, 16 * 4], F32),
        "convb_pack": p.din("convb_pack", [128, 16], F32),
        "dtb_pack": p.din("dtb_pack", [128, 16], F32),
        "lng_pack": p.din("lng_pack", [128, 8], F32),
        "lnb_pack": p.din("lnb_pack", [128, 8], F32),
    }
    mk_out = p.dout if ext else p.dscratch
    io["xi_pre"] = mk_out("xi_pre", [DI, T], BF16)
    io["z_bf"] = mk_out("z_bf", [DI, T], BF16)
    for d in "fr":
        io[f"u_{d}"] = mk_out(f"u_{d}", [DI, T], BF16)
        io[f"delta_{d}"] = mk_out(f"delta_{d}", [DI, T], F32)
        io[f"bc_{d}"] = mk_out(f"bc_{d}", [32, T], F32)
    return io


def emit_A(p, io):
    xT = io["xT"]
    wip_sb = load_w_cast(p, "wip_sb", io["w_inproj"], 8, 2 * DI)
    wxp_sb = load_w_cast(p, "wxp_sb", io["w_xproj"], 16, 96)
    wdt_sb = p.sb("wdt_sb", [64, DI], BF16)
    p.dma(wdt_sb[:], io["w_dtproj"], engine="gpsimd")
    cw_sb = p.sb("cw_sb", [128, 64], F32); p.dma(cw_sb[:], io["convw_pack"])
    cb_sb = p.sb("cb_sb", [128, 16], F32); p.dma(cb_sb[:], io["convb_pack"])
    dtb_sb = p.sb("dtb_sb", [128, 16], F32); p.dma(dtb_sb[:], io["dtb_pack"])
    lng_sb = p.sb("lng_sb", [128, 8], F32); p.dma(lng_sb[:], io["lng_pack"])
    lnb_sb = p.sb("lnb_sb", [128, 8], F32); p.dma(lnb_sb[:], io["lnb_pack"])

    x_sb = p.sb("x_sb", [128, 8 * TC], F32)
    xn_bf = p.sb("xn_bf", [128, 8 * TC], BF16)
    mm_ps = p.pp[0]
    ev_bf = p.sb("ev_bf", [128, TC], BF16)

    xTr = xT.rearrange("(c p) t -> p c t", p=128)

    for tc in range(NTC):
        tsl = slice(tc * TC, (tc + 1) * TC)
        p.dma(rearr3(x_sb, 8, TC), xTr[:, :, tsl])
        xch = [x_sb[:, c * TC:(c + 1) * TC] for c in range(8)]
        och = [xn_bf[:, c * TC:(c + 1) * TC] for c in range(8)]
        emit_ln(p, xch, och, TC, f"ln{tc}",
                g_ap=[lng_sb[:, c:c + 1] for c in range(8)],
                b_ap=[lnb_sb[:, c:c + 1] for c in range(8)])
        wv = rearr3(wip_sb, 8, 2 * DI)
        for m in range(32):
            for c in range(8):
                p.mm(mm_ps[:], wv[:, c, m * 128:(m + 1) * 128],
                     xn_bf[:, c * TC:(c + 1) * TC],
                     start=(c == 0), stop=(c == 7))
            p.acopy(ev_bf[:], mm_ps[:])
            dst = io["xi_pre"] if m < 16 else io["z_bf"]
            mm_loc = m if m < 16 else m - 16
            p.dma(dst[mm_loc * 128:(mm_loc + 1) * 128, tsl], ev_bf[:])
        p.seq.barrier()

    halo = p.sb("halo", [128, TC + 3], BF16)
    acc = p.sb("acc", [128, TC], F32)
    u_sb = p.sb("u_sb", [128, 16 * TC], BF16)
    dbc_ps = p.pp[1][0:96, :]
    dt_bf = p.sb("dt_bf", [64, TC], BF16)
    bc_sb = p.sb("bc_sb", [32, TC], F32)
    sp1 = p.sb("sp1", [128, TC], F32)
    sp2 = p.sb("sp2", [128, TC], F32)
    delta_sb = p.sb("delta_sb", [128, TC], F32)

    for d in "fr":
        for tc in range(NTC):
            tsl = slice(tc * TC, (tc + 1) * TC)
            for c in range(16):
                rows = slice(c * 128, (c + 1) * 128)
                if d == "f":
                    lo = tc * TC - 3
                    if lo < 0:
                        p.memset(halo[:, 0:3], 0.0)
                        p.dma(halo[:, 3:TC + 3], io["xi_pre"][rows, 0:TC])
                    else:
                        p.dma(halo[:], io["xi_pre"][rows, lo:tc * TC + TC])
                    offs = [(3, 3), (2, 2), (1, 1), (0, 0)]
                else:
                    hi = tc * TC + TC + 3
                    if hi > T:
                        p.dma(halo[:, 0:TC], io["xi_pre"][rows, tsl])
                        p.memset(halo[:, TC:TC + 3], 0.0)
                    else:
                        p.dma(halo[:], io["xi_pre"][rows, tc * TC:hi])
                    offs = [(3, 0), (2, 1), (1, 2), (0, 3)]
                (k0, o0) = offs[0]
                p.ts(acc[:], halo[:, o0:o0 + TC], cw_sb[:, c * 4 + k0:c * 4 + k0 + 1],
                     cb_sb[:, c:c + 1], ALU.mult, ALU.add)
                for (k, o) in offs[1:]:
                    p.stt(acc[:], halo[:, o:o + TC], cw_sb[:, c * 4 + k:c * 4 + k + 1],
                          acc[:], ALU.mult, ALU.add)
                usl = u_sb[:, c * TC:(c + 1) * TC]
                p.act(usl, acc[:], AF.Silu)
                p.dma(io[f"u_{d}"][rows, tsl], usl)
            wxv = rearr3(wxp_sb, 16, 96)
            for c in range(16):
                p.mm(dbc_ps[:], wxv[:, c, :], u_sb[:, c * TC:(c + 1) * TC],
                     start=(c == 0), stop=(c == 15))
            p.acopy(dt_bf[:], dbc_ps[0:64, :])
            p.acopy(bc_sb[:], dbc_ps[64:96, :])
            p.dma(io[f"bc_{d}"][:, tsl], bc_sb[:])
            for m in range(16):
                p.mm(mm_ps[:], wdt_sb[:, m * 128:(m + 1) * 128], dt_bf[:])
                b_ap = dtb_sb[:, m:m + 1]
                p.act(sp1[:], mm_ps[:], AF.Abs, bias=b_ap)
                p.act(sp2[:], sp1[:], AF.Exp, scale=-1.0)
                p.act(sp1[:], sp2[:], AF.Ln, bias=1.0)
                p.ts(sp2[:], mm_ps[:], b_ap, 0.0, ALU.add, ALU.max)
                p.tt(delta_sb[:], sp2[:], sp1[:], ALU.add)
                p.dma(io[f"delta_{d}"][m * 128:(m + 1) * 128, tsl], delta_sb[:])
            p.seq.barrier()


def build_launch_A(strict=False):
    p = Prog("mamba_pre", strict=strict)
    io = declare_A_io(p)
    p.setup_consts()
    emit_A(p, io)
    return p


def stream_w_colblock(p, name, hbm_ap, kchunks, col0, ncols, dt=BF16,
                      bufcols=None):
    bufcols = bufcols or ncols
    t = p.sb(name, [128, kchunks * bufcols], dt)
    src = hbm_ap.rearrange("(c p) m -> p c m", p=128)[:, :, col0:col0 + ncols]
    dst = rearr3(t, kchunks, bufcols)[:, :, 0:ncols]
    p.dma(dst, src, engine="gpsimd")
    return t


def emit_scan_pass(p, d, aps):
    rev = (d == "r")
    delta_sb = p.sb("sc_delta", [128, 16 * TC], F32)
    u_sb = p.sb("sc_u", [128, 16 * TC], BF16)
    du_sb = p.sb("sc_du", [128, 16 * TC], BF16)
    y_sb = p.sb("sc_y", [128, 16 * TC], F32)
    ys_bf = p.sb("sc_ys", [128, 16 * TC], BF16)
    bcs = p.sb("sc_bcs", [1, 32 * TC], BF16)
    a_sb = p.sb("sc_a", [128, TC], F32)
    b_sb = p.sb("sc_b", [128, TC], F32)
    h_sb = p.sb("sc_h", [128, TC], F32)
    g_sb = p.sb("sc_g", [128, TC], F32)
    zc = p.sb("sc_zc", [128, TC], BF16)
    sz = p.sb("sc_sz", [128, TC], F32)
    hst = p.sb("sc_hst", [128, 256], F32)
    ev_bf = p.sb("sc_ev", [128, TC], BF16)
    p.memset(hst[:], 0.0)

    deltaR = aps["delta"].rearrange("(c p) t -> p c t", p=128)
    uR = aps["u"].rearrange("(c p) t -> p c t", p=128)
    zR = aps["z"].rearrange("(c p) t -> p c t", p=128)
    tcs = list(range(NTC - 1, -1, -1)) if rev else list(range(NTC))

    for tc in tcs:
        tsl = slice(tc * TC, (tc + 1) * TC)
        p.dma(rearr3(delta_sb, 16, TC), deltaR[:, :, tsl])
        p.dma(rearr3(u_sb, 16, TC), uR[:, :, tsl])
        bcsrc = aps["bc"][:, tsl].unsqueeze(0)
        p.dma(bcs[:].rearrange("p (r t) -> p r t", r=32), bcsrc,
              engine="gpsimd")
        for dc in range(16):
            csl = slice(dc * TC, (dc + 1) * TC)
            p.tt(du_sb[:, csl], delta_sb[:, csl], u_sb[:, csl], ALU.mult)
        for n in range(NST):
            p.mm(p.pp[2][:], p.ones_r_bf[:], bcs[0:1, n * TC:(n + 1) * TC])
            p.mm(p.pp[3][:], p.ones_r_bf[:], bcs[0:1, (16 + n) * TC:(17 + n) * TC])
            for dc in range(16):
                csl = slice(dc * TC, (dc + 1) * TC)
                p.act(a_sb[:], delta_sb[:, csl], AF.Exp,
                      scale=p.A_pack_sb[:, dc * NST + n:dc * NST + n + 1])
                p.tt(b_sb[:], du_sb[:, csl], p.pp[2][:], ALU.mult)
                if rev:
                    p.tts(h_sb[:, ::-1], a_sb[:, ::-1], b_sb[:, ::-1],
                          hst[:, dc * NST + n:dc * NST + n + 1], ALU.mult, ALU.add)
                    p.vcopy(hst[:, dc * NST + n:dc * NST + n + 1], h_sb[:, 0:1])
                else:
                    p.tts(h_sb[:], a_sb[:], b_sb[:],
                          hst[:, dc * NST + n:dc * NST + n + 1], ALU.mult, ALU.add)
                    p.vcopy(hst[:, dc * NST + n:dc * NST + n + 1],
                            h_sb[:, TC - 1:TC])
                p.tt(g_sb[:], h_sb[:], p.pp[3][:], ALU.mult)
                if n == 0:
                    p.vcopy(y_sb[:, csl], g_sb[:])
                else:
                    p.tt(y_sb[:, csl], y_sb[:, csl], g_sb[:], ALU.add)
        for dc in range(16):
            csl = slice(dc * TC, (dc + 1) * TC)
            p.stt(y_sb[:, csl], u_sb[:, csl], p.D_pack_sb[:, dc:dc + 1],
                  y_sb[:, csl], ALU.mult, ALU.add)
            p.dma(zc[:], zR[:, dc, tsl])
            p.act(sz[:], zc[:], AF.Silu)
            p.tt(ys_bf[:, csl], y_sb[:, csl], sz[:], ALU.mult)
        for oc in range(8):
            wt = stream_w_colblock(p, "sc_wop", aps["wop"], 16, oc * 128, 128)
            wv = rearr3(wt, 16, 128)
            for dc in range(16):
                p.mm(p.pp[0][:], wv[:, dc, :], ys_bf[:, dc * TC:(dc + 1) * TC],
                     start=(dc == 0), stop=(dc == 15))
            p.acopy(ev_bf[:], p.pp[0][:])
            p.dma(aps["y_dst"][oc * 128:(oc + 1) * 128, tsl], ev_bf[:])
        p.seq.barrier()


def emit_B_common(p, io):
    t = p.sb("alog_sb", [128, 256], F32)
    p.dma(t[:], io["alog_pack"])
    p.A_pack_sb = p.sb("A_pack_sb", [128, 256], F32)
    p.act(p.A_pack_sb[:], t[:], AF.Exp)
    p.ts(p.A_pack_sb[:], p.A_pack_sb[:], -1.0, None, ALU.mult)
    p.D_pack_sb = p.sb("D_pack_sb", [128, 16], F32)
    p.dma(p.D_pack_sb[:], io["d_pack"])


def emit_B2_post(p, io):
    drb_sb = p.sb("drb_sb", [128, 8], F32); p.dma(drb_sb[:], io["drb_pack"])
    glb_sb = p.sb("glb_sb", [128, 16], F32); p.dma(glb_sb[:], io["glb_pack"])
    yf_sb = p.sb("sc_u", [128, 16 * TC], BF16)[:, 0:8 * TC]
    yr_sb = p.sb("sc_u", [128, 16 * TC], BF16)[:, 8 * TC:16 * TC]
    gl_bf = p.sb("sc_du", [128, 16 * TC], BF16)[:, 0:8 * TC]
    ga_bf = p.sb("sc_du", [128, 16 * TC], BF16)[:, 8 * TC:16 * TC]
    sg = p.sb("sc_a", [128, TC], F32)[:]
    mo = p.sb("sc_y", [128, 16 * TC], F32)[:, 0:8 * TC]
    x_sb = p.sb("sc_y", [128, 16 * TC], F32)[:, 8 * TC:16 * TC]
    x1_sb = p.sb("sc_delta", [128, 16 * TC], F32)[:, 0:8 * TC]
    yfR = io["y_f"].rearrange("(c p) t -> p c t", p=128)
    yrR = io["y_r"].rearrange("(c p) t -> p c t", p=128)
    xTr = io["xT"].rearrange("(c p) t -> p c t", p=128)
    wdr, wglu = io["w_dimred"], io["w_glu"]
    for tc in range(NTC):
        tsl = slice(tc * TC, (tc + 1) * TC)
        p.dma(rearr3(yf_sb, 8, TC), yfR[:, :, tsl])
        p.dma(rearr3(yr_sb, 8, TC), yrR[:, :, tsl])
        p.dma(rearr3(x_sb, 8, TC), xTr[:, :, tsl])
        for oc in range(8):
            wt = stream_w_colblock(p, "po_wdr", wdr, 16, oc * 128, 128)
            wv = rearr3(wt, 16, 128)
            for c in range(8):
                p.mm(p.pp[0][:], wv[:, c, :], yf_sb[:, c * TC:(c + 1) * TC],
                     start=(c == 0), stop=False)
            for c in range(8):
                p.mm(p.pp[0][:], wv[:, 8 + c, :], yr_sb[:, c * TC:(c + 1) * TC],
                     start=False, stop=(c == 7))
            p.act(gl_bf[:, oc * TC:(oc + 1) * TC], p.pp[0][:], AF.Gelu,
                  bias=drb_sb[:, oc:oc + 1])
        for m in range(16):
            wt = stream_w_colblock(p, "po_wglu", wglu, 8, m * 128, 128)
            wv = rearr3(wt, 8, 128)
            for c in range(8):
                p.mm(p.pp[0][:], wv[:, c, :], gl_bf[:, c * TC:(c + 1) * TC],
                     start=(c == 0), stop=(c == 7))
            if m < 8:
                p.ts(ga_bf[:, m * TC:(m + 1) * TC], p.pp[0][:],
                     glb_sb[:, m:m + 1], None, ALU.add)
            else:
                mm2 = m - 8
                p.act(sg[:], p.pp[0][:], AF.Sigmoid, bias=glb_sb[:, m:m + 1])
                csl = slice(mm2 * TC, (mm2 + 1) * TC)
                p.tt(mo[:, csl], ga_bf[:, csl], sg[:], ALU.mult)
                p.stt(mo[:, csl], x_sb[:, csl], 2.0, mo[:, csl], ALU.mult, ALU.add)
        xch = [mo[:, c * TC:(c + 1) * TC] for c in range(8)]
        o32 = [x1_sb[:, c * TC:(c + 1) * TC] for c in range(8)]
        emit_ln(p, xch, None, TC, "pln", out_chunks_f32=o32)
        for c in range(8):
            p.dma(io["x1T"][c * 128:(c + 1) * 128, tsl],
                  x1_sb[:, c * TC:(c + 1) * TC])
        p.seq.barrier()


def build_launch_B1(strict=False):
    p = Prog("scan_f", strict=strict)
    aps = {
        "delta": p.din("delta_f", [DI, T], F32),
        "u": p.din("u_f", [DI, T], BF16),
        "bc": p.din("bc_f", [32, T], F32),
        "z": p.din("z_bf", [DI, T], BF16),
        "wop": p.din("w_outproj", [DI, D], F32),
        "y_dst": p.dout("y_f", [D, T], BF16),
    }
    io = {"alog_pack": p.din("alog_pack", [128, 256], F32),
          "d_pack": p.din("d_pack", [128, 16], F32)}
    p.setup_consts()
    emit_B_common(p, io)
    emit_scan_pass(p, "f", aps)
    return p


def build_launch_B2(strict=False):
    p = Prog("scan_r_post", strict=strict)
    aps = {
        "delta": p.din("delta_r", [DI, T], F32),
        "u": p.din("u_r", [DI, T], BF16),
        "bc": p.din("bc_r", [32, T], F32),
        "z": p.din("z_bf", [DI, T], BF16),
        "wop": p.din("w_outproj", [DI, D], F32),
        "y_dst": p.dout("y_r", [D, T], BF16),
    }
    io = {
        "alog_pack": p.din("alog_pack", [128, 256], F32),
        "d_pack": p.din("d_pack", [128, 16], F32),
        "xT": p.din("xT", [D, T], F32),
        "y_f": p.din("y_f", [D, T], BF16),
        "w_dimred": p.din("w_dimred", [2 * D, D], F32),
        "w_glu": p.din("w_glu", [D, 2 * D], F32),
        "drb_pack": p.din("drb_pack", [128, 8], F32),
        "glb_pack": p.din("glb_pack", [128, 16], F32),
        "x1T": p.dout("x1T", [D, T], F32),
    }
    io["y_r"] = aps["y_dst"]
    p.setup_consts()
    emit_B_common(p, io)
    emit_scan_pass(p, "r", aps)
    p.seq.barrier()
    emit_B2_post(p, io)
    return p


def prep_B_weights(inp):
    w = {}
    w["w_outproj"] = np.ascontiguousarray(np.asarray(inp["L0_attn_out_proj_w"]).T)
    w["w_dimred"] = np.ascontiguousarray(np.asarray(inp["L0_attn_dim_reduce_w"]).T)
    w["w_glu"] = np.ascontiguousarray(np.asarray(inp["L0_attn_glu_w"]).T)
    al = np.asarray(inp["L0_attn_A_log"], np.float32)
    w["alog_pack"] = np.ascontiguousarray(
        al.reshape(16, 128, 16).transpose(1, 0, 2).reshape(128, 256))
    w["d_pack"] = _pack_cols(inp["L0_attn_D"], 16)
    w["drb_pack"] = _pack_cols(inp["L0_attn_dim_reduce_b"], 8)
    w["glb_pack"] = _pack_cols(inp["L0_attn_glu_b"], 16)
    return w


def emit_C(p, io):
    x1, gw, w1, w2, ws1, ws2 = (io["x1T"], io["w_gate"], io["w_exp1"],
                                io["w_exp2"], io["w_sh1"], io["w_sh2"])
    idn8 = p.sb("idn8", [8, 8], F32)
    idn128 = p.sb("idn128", [128, 128], F32)
    i8h = p.inline(np.eye(8, dtype=np.float32), "i8c")
    i128h = p.inline(np.eye(128, dtype=np.float32), "i128c")
    p.dma(idn8[:], i8h.ap())
    p.dma(idn128[:], i128h.ap())
    gw_sb = p.sb("gw_sb", [128, 8 * 8], F32)
    p.dma(rearr3(gw_sb, 8, 8), gw.rearrange("(c p) m -> p c m", p=128))
    wscr = io["w_scr"]

    x_sb = p.sb("c_x", [128, 8 * TC], F32)
    xb_sb = p.sb("c_xb", [128, 8 * TC], BF16)
    sc_sb = p.sb("c_sc", [8, TC], F32)
    tsc = p.sb("c_tsc", [128, 8 * 4], F32)
    m1n = p.sb("c_m1n", [128, 1], F32)
    m2n = p.sb("c_m2n", [128, 1], F32)
    rs = p.sb("c_rs", [128, 1], F32)
    iseq = p.sb("c_iseq", [128, 8], F32)
    sc2 = p.sb("c_sc2", [128, 8], F32)
    wfull = p.sb("c_wfull", [128, 8], F32)
    keep = p.sb("c_keep", [128, 8], F32)
    wtok = p.sb("c_wtok", [128, 8 * 4], F32)
    wT_sb = p.sb("c_wT", [8, TC], F32)
    wstrip = p.sb("c_wstrip", [1, 8 * TC], F32)
    ya = p.sb("c_ya", [128, 6 * TC], BF16)
    hg = p.sb("c_hg", [128, 6 * TC], BF16)
    sg = p.sb("c_sg", [128, TC], F32)
    zacc = p.sb("c_zacc", [128, 8 * TC], F32)
    ev = p.sb("c_ev", [128, TC], F32)
    ya2 = p.sb("c_ya2", [128, 12 * TC], BF16)
    hg2 = p.sb("c_hg2", [128, 12 * TC], BF16)
    x2_sb = p.sb("c_x2", [128, 8 * TC], F32)

    x1R = x1.rearrange("(c p) t -> p c t", p=128)
    for tc in range(NTC):
        tsl = slice(tc * TC, (tc + 1) * TC)
        p.dma(rearr3(x_sb, 8, TC), x1R[:, :, tsl])
        for c in range(8):
            p.vcopy(xb_sb[:, c * TC:(c + 1) * TC], x_sb[:, c * TC:(c + 1) * TC])
        gv = rearr3(gw_sb, 8, 8)
        for c in range(8):
            p.mm(p.pp[1][0:8, :], gv[:, c, :], x_sb[:, c * TC:(c + 1) * TC],
                 start=(c == 0), stop=(c == 7))
        p.acopy(sc_sb[:], p.pp[1][0:8, :])
        for q in range(4):
            p.trans(p.pp[2][:, 0:8], sc_sb[:, q * 128:(q + 1) * 128], idn8[:])
            p.acopy(tsc[:, q * 8:(q + 1) * 8], p.pp[2][:, 0:8])
        for q in range(4):
            s8 = tsc[:, q * 8:(q + 1) * 8]
            p.reduce(m1n[:], s8, ALU.max, negate=True)
            p.act(wfull[:], s8, AF.Exp, bias=m1n[:], accum_out=rs[:])
            p.recip(rs[:], rs[:])
            p.ts(wfull[:], wfull[:], rs[:], None, ALU.mult)
            p.ts(iseq[:], s8, m1n[:], 0.0, ALU.add, ALU.is_ge)
            p.stt(sc2[:], iseq[:], -1e30, s8, ALU.mult, ALU.add)
            p.reduce(m2n[:], sc2[:], ALU.max, negate=True)
            p.ts(keep[:], s8, m2n[:], 0.0, ALU.add, ALU.is_ge)
            p.tt(wtok[:, q * 8:(q + 1) * 8], wfull[:], keep[:], ALU.mult)
        for q in range(4):
            p.trans(p.pp[2][0:8, 0:128], wtok[:, q * 8:(q + 1) * 8], idn128[:])
            p.acopy(wT_sb[:, q * 128:(q + 1) * 128], p.pp[2][0:8, 0:128])
        p.dma(wscr[:, :], wT_sb[:])
        p.dma(wstrip[:].rearrange("p (r t) -> p r t", r=8), wscr[:, :].unsqueeze(0))
        for e in range(8):
            w1e = p.sb("c_w1e", [128, 8 * 1536], BF16)
            p.dma(rearr3(w1e, 8, 1536),
                  w1[e].rearrange("(c p) m -> p c m", p=128), engine="gpsimd")
            w2e = p.sb("c_w2e", [128, 6 * D], BF16)
            p.dma(rearr3(w2e, 6, D),
                  w2[e].rearrange("(c p) m -> p c m", p=128), engine="gpsimd")
            w1v = rearr3(w1e, 8, 1536)
            for k in range(6):
                for c in range(8):
                    p.mm(p.pp[0][:], w1v[:, c, k * 128:(k + 1) * 128],
                         xb_sb[:, c * TC:(c + 1) * TC],
                         start=(c == 0), stop=(c == 7))
                p.acopy(ya[:, k * TC:(k + 1) * TC], p.pp[0][:])
            p.mm(p.pp[3][:], p.ones_r_f32[:], wstrip[0:1, e * TC:(e + 1) * TC])
            for k in range(6):
                for c in range(8):
                    p.mm(p.pp[0][:], w1v[:, c, 768 + k * 128:768 + (k + 1) * 128],
                         xb_sb[:, c * TC:(c + 1) * TC],
                         start=(c == 0), stop=(c == 7))
                p.act(sg[:], p.pp[0][:], AF.Silu)
                p.tt(sg[:], sg[:], p.pp[3][:], ALU.mult)
                p.tt(hg[:, k * TC:(k + 1) * TC], ya[:, k * TC:(k + 1) * TC],
                     sg[:], ALU.mult)
            w2v = rearr3(w2e, 6, D)
            for oc in range(8):
                for k in range(6):
                    p.mm(p.pp[0][:], w2v[:, k, oc * 128:(oc + 1) * 128],
                         hg[:, k * TC:(k + 1) * TC],
                         start=(k == 0), stop=(k == 5))
                if e == 0:
                    p.acopy(zacc[:, oc * TC:(oc + 1) * TC], p.pp[0][:])
                else:
                    p.acopy(ev[:], p.pp[0][:])
                    p.tt(zacc[:, oc * TC:(oc + 1) * TC],
                         zacc[:, oc * TC:(oc + 1) * TC], ev[:], ALU.add)
        for k in range(12):
            wt = stream_w_colblock(p, "c_ws1a", ws1, 8, k * 128, 128)
            wv = rearr3(wt, 8, 128)
            for c in range(8):
                p.mm(p.pp[0][:], wv[:, c, :], xb_sb[:, c * TC:(c + 1) * TC],
                     start=(c == 0), stop=(c == 7))
            p.acopy(ya2[:, k * TC:(k + 1) * TC], p.pp[0][:])
        for k in range(12):
            wt = stream_w_colblock(p, "c_ws1b", ws1, 8, 1536 + k * 128, 128)
            wv = rearr3(wt, 8, 128)
            for c in range(8):
                p.mm(p.pp[0][:], wv[:, c, :], xb_sb[:, c * TC:(c + 1) * TC],
                     start=(c == 0), stop=(c == 7))
            p.act(sg[:], p.pp[0][:], AF.Silu)
            p.tt(hg2[:, k * TC:(k + 1) * TC], ya2[:, k * TC:(k + 1) * TC],
                 sg[:], ALU.mult)
        for oc in range(8):
            wt = stream_w_colblock(p, "c_ws2", ws2, 12, oc * 128, 128)
            wv = rearr3(wt, 12, 128)
            for k in range(12):
                p.mm(p.pp[0][:], wv[:, k, :], hg2[:, k * TC:(k + 1) * TC],
                     start=(k == 0), stop=(k == 11))
            p.acopy(ev[:], p.pp[0][:])
            p.tt(zacc[:, oc * TC:(oc + 1) * TC],
                 zacc[:, oc * TC:(oc + 1) * TC], ev[:], ALU.add)
        for c in range(8):
            p.tt(zacc[:, c * TC:(c + 1) * TC], zacc[:, c * TC:(c + 1) * TC],
                 x_sb[:, c * TC:(c + 1) * TC], ALU.add)
        zch = [zacc[:, c * TC:(c + 1) * TC] for c in range(8)]
        o32 = [x2_sb[:, c * TC:(c + 1) * TC] for c in range(8)]
        emit_ln(p, zch, None, TC, "cln", out_chunks_f32=o32)
        for c in range(8):
            p.dma(io["x2T"][c * 128:(c + 1) * 128, tsl],
                  x2_sb[:, c * TC:(c + 1) * TC])
        p.seq.barrier()


def build_launch_C(strict=False):
    p = Prog("moe", strict=strict)
    io = {
        "x1T": p.din("x1T", [D, T], F32),
        "w_gate": p.din("w_gate", [D, 8], F32),
        "w_exp1": p.din("w_exp1", [8, D, 1536], F32),
        "w_exp2": p.din("w_exp2", [8, 768, D], F32),
        "w_sh1": p.din("w_sh1", [D, 3072], F32),
        "w_sh2": p.din("w_sh2", [1536, D], F32),
        "x2T": p.dout("x2T", [D, T], F32),
    }
    p.setup_consts()
    io["w_scr"] = p.dscratch("w_scr", [8, TC], F32)
    emit_C(p, io)
    return p


def prep_C_weights(inp):
    w = {}
    w["w_gate"] = np.ascontiguousarray(np.asarray(inp["L0_ffn_gate_w"]).T)
    w["w_exp1"] = np.ascontiguousarray(
        np.asarray(inp["L0_ffn_W1"]).transpose(0, 2, 1))
    w["w_exp2"] = np.ascontiguousarray(
        np.asarray(inp["L0_ffn_W2"]).transpose(0, 2, 1))
    w["w_sh1"] = np.ascontiguousarray(np.asarray(inp["L0_ffn_Ws1"]).T)
    w["w_sh2"] = np.ascontiguousarray(np.asarray(inp["L0_ffn_Ws2"]).T)
    return w


def emit_D(p, io):
    x2, wqkv, wo = io["x2T"], io["w_qkvT"], io["w_attnoT"]
    qkv_sb = load_w_cast(p, "d_qkv", wqkv, 8, 3 * D)
    ow_sb = load_w_cast(p, "d_ow", wo, 8, D)
    bq_sb = p.sb("d_bq", [128, 16], F32); p.dma(bq_sb[:], io["qb_pack"])
    ob_sb = p.sb("d_ob", [128, 8], F32); p.dma(ob_sb[:], io["ob_pack"])
    brow_sb = p.sb("d_brow", [1, 3 * D], BF16)
    p.dma(brow_sb[:], io["qkvb_row"], engine="gpsimd")
    trimask = p.inline(np.triu(np.ones((128, 128), np.float32)), "triu128")
    tri_sb = p.sb("d_tri", [128, 128], F32)
    p.dma(tri_sb[:], trimask.ap())
    eps_c = p.const_col(1e-6)

    x_sb = p.sb("d_x", [128, 8 * TC], F32)
    xb_sb = p.sb("d_xb", [128, 8 * TC], BF16)
    qT = p.sb("d_qT", [128, 8 * TC], BF16)
    kT = p.sb("d_kT", [128, 8 * TC], BF16)
    ktok = p.sb("d_ktok", [128, 4 * D], BF16)
    vtok = p.sb("d_vtok", [128, 4 * D], BF16)
    attnT = p.sb("d_attnT", [128, 8 * TC], BF16)
    KVa = p.sb("d_KVa", [128, 8 * 65], F32)
    KVb = p.sb("d_KVb", [128, 8 * 65], BF16)
    Sm = p.sb("d_Sm", [128, 128], BF16)
    na = p.sb("d_na", [64, 128], F32)
    rd = p.sb("d_rd", [1, 128], F32)
    rd2 = p.sb("d_rd2", [1, 128], F32)
    x3_sb = p.sb("d_x3", [128, 8 * TC], F32)
    ev = p.sb("d_ev", [128, TC], F32)
    p.memset(KVa[:], 0.0)
    p.memset(KVb[:], 0.0)

    qv = rearr3(qkv_sb, 8, 3 * D)
    ov = rearr3(ow_sb, 8, D)
    x2R = x2.rearrange("(c p) t -> p c t", p=128)
    for tc in range(NTC):
        tsl = slice(tc * TC, (tc + 1) * TC)
        p.dma(rearr3(x_sb, 8, TC), x2R[:, :, tsl])
        for c in range(8):
            p.vcopy(xb_sb[:, c * TC:(c + 1) * TC], x_sb[:, c * TC:(c + 1) * TC])
        for which, dst in ((0, qT), (1, kT)):
            for jc in range(8):
                for c in range(8):
                    p.mm(p.pp[0][:], qv[:, c, which * D + jc * 128:
                                        which * D + (jc + 1) * 128],
                         xb_sb[:, c * TC:(c + 1) * TC],
                         start=(c == 0), stop=(c == 7))
                dsl = dst[:, jc * TC:(jc + 1) * TC]
                p.ts(dsl, p.pp[0][:], bq_sb[:, which * 8 + jc:which * 8 + jc + 1],
                     0.0, ALU.add, ALU.max)
                p.ts(dsl, dsl, eps_c, None, ALU.add)
        for sub in range(4):
            for which, dst in ((1, ktok), (2, vtok)):
                for eh in range(2):
                    for c in range(8):
                        p.mm(p.pp[0][:], xb_sb[:, c * TC + sub * 128:
                                               c * TC + sub * 128 + 128],
                             qv[:, c, which * D + eh * 512:which * D + (eh + 1) * 512],
                             start=(c == 0), stop=False)
                    p.mm(p.pp[0][:], p.ones_r_bf[:],
                         brow_sb[0:1, which * D + eh * 512:which * D + (eh + 1) * 512],
                         start=False, stop=True)
                    dsl = dst[:, sub * D + eh * 512:sub * D + (eh + 1) * 512]
                    if which == 1:
                        p.ts(dsl, p.pp[0][:], 0.0, eps_c, ALU.max, ALU.add)
                    else:
                        p.acopy(dsl, p.pp[0][:])
        for sub in range(4):
            for h in range(16):
                hp = 64 * (h % 2)
                hc = h // 2
                qsl = qT[hp:hp + 64, hc * TC + sub * 128:hc * TC + sub * 128 + 128]
                ksl = kT[hp:hp + 64, hc * TC + sub * 128:hc * TC + sub * 128 + 128]
                ktk = ktok[:, sub * D + h * 64:sub * D + (h + 1) * 64]
                vtk = vtok[:, sub * D + h * 64:sub * D + (h + 1) * 64]
                p.mm(p.pp[0][:, 0:128], ksl, qsl)
                p.tt(Sm[:], p.pp[0][:, 0:128], tri_sb[:], ALU.mult)
                kva = KVa[hp:hp + 64, hc * 65:(hc + 1) * 65]
                kvb = KVb[hp:hp + 64, hc * 65:(hc + 1) * 65]
                kvb_kv = KVb[hp:hp + 64, hc * 65:hc * 65 + 64]
                kvb_kc = KVb[hp:hp + 64, hc * 65 + 64:hc * 65 + 65]
                p.mm(p.pp[1][0:64, 0:128], vtk, Sm[:], start=True, stop=False)
                p.mm(p.pp[1][0:64, 0:128], kvb_kv, qsl, start=False, stop=True)
                p.mm(p.pp[3][0:1, 0:128], p.ones_c_bf[:], Sm[:],
                     start=True, stop=False)
                p.mm(p.pp[3][0:1, 0:128], kvb_kc, qsl, start=False, stop=True)
                p.mm(p.pp[2][0:64, 0:64], ktk, vtk, start=True, stop=True)
                p.mm(p.pp[2][0:64, 64:65], ktk, p.ones_c_bf[:],
                     start=True, stop=True)
                p.ts(rd[:], p.pp[3][0:1, 0:128], eps_c[0:1, :], None, ALU.max)
                p.recip(rd2[:], rd[:])
                p.acopy(na[:], p.pp[1][0:64, 0:128])
                p.mm(p.pp[3][0:64, 0:128], p.ones_r_f32[0:1, 0:64], rd2[:],
                     start=True, stop=True)
                asl = attnT[hp:hp + 64, hc * TC + sub * 128:hc * TC + sub * 128 + 128]
                p.tt(asl, na[:], p.pp[3][0:64, 0:128], ALU.mult)
                p.tt(kva, kva, p.pp[2][0:64, 0:65], ALU.add)
                p.vcopy(kvb, kva)
        for oc in range(8):
            for ec in range(8):
                p.mm(p.pp[0][:], ov[:, ec, oc * 128:(oc + 1) * 128],
                     attnT[:, ec * TC:(ec + 1) * TC],
                     start=(ec == 0), stop=(ec == 7))
            p.ts(ev[:], p.pp[0][:], ob_sb[:, oc:oc + 1], None, ALU.add)
            p.tt(x3_sb[:, oc * TC:(oc + 1) * TC], ev[:],
                 x_sb[:, oc * TC:(oc + 1) * TC], ALU.add)
        xch = [x3_sb[:, c * TC:(c + 1) * TC] for c in range(8)]
        emit_ln(p, xch, None, TC, "dln", out_chunks_f32=xch)
        for c in range(8):
            p.dma(io["x3T"][c * 128:(c + 1) * 128, tsl],
                  x3_sb[:, c * TC:(c + 1) * TC])
        p.seq.barrier()


def build_launch_D(strict=False):
    p = Prog("linattn", strict=strict)
    io = {
        "x2T": p.din("x2T", [D, T], F32),
        "w_qkvT": p.din("w_qkvT", [D, 3 * D], F32),
        "qb_pack": p.din("qb_pack", [128, 16], F32),
        "qkvb_row": p.din("qkvb_row", [1, 3 * D], F32),
        "w_attnoT": p.din("w_attnoT", [D, D], F32),
        "ob_pack": p.din("ob_pack", [128, 8], F32),
        "x3T": p.dout("x3T", [D, T], F32),
    }
    p.setup_consts()
    emit_D(p, io)
    return p


def prep_D_weights(inp):
    w = {}
    w["w_qkvT"] = np.ascontiguousarray(np.asarray(inp["L1_attn_qkv_w"]).T)
    qb = np.asarray(inp["L1_attn_qkv_b"], np.float32)
    w["qb_pack"] = np.concatenate(
        [_pack_cols(qb[0:D], 8), _pack_cols(qb[D:2 * D], 8)], axis=1)
    w["qkvb_row"] = np.ascontiguousarray(qb.reshape(1, 3 * D))
    w["w_attnoT"] = np.ascontiguousarray(np.asarray(inp["L1_attn_out_w"]).T)
    w["ob_pack"] = _pack_cols(inp["L1_attn_out_b"], 8)
    return w


DIM_INNER = 2730
NHK1 = 22


def emit_E(p, io):
    xs, wq, wo, w1, w2 = io["xsT"], io["w_lqkv"], io["w_lo"], io["w_l1"], io["w_l2"]
    m0 = np.triu(np.ones((128, 128), np.float32))
    m2 = np.tril(np.ones((128, 128), np.float32))
    mb0 = ((1 - m0) * -1e30).astype(np.float32)
    mb1 = np.zeros((128, 128), np.float32)
    mb2 = ((1 - m2) * -1e30).astype(np.float32)
    mbias_sb = p.sb("e_mbias", [128, 384], F32)
    p.dma(mbias_sb[:], p.inline(np.concatenate([mb0, mb1, mb2], 1), "e_mbias_c").ap())
    m01T = np.concatenate([m0.T, np.ones((128, 128), np.float32), m2.T], 1)
    m01T_sb = p.sb("e_m01T", [128, 384], F16)
    p.dma(m01T_sb[:], p.inline(m01T.astype(np.float16), "e_m01T_c").ap(), engine="gpsimd")
    rotm = np.zeros((128, 128), np.float32)
    for base in (0, 64):
        for i in range(32):
            rotm[base + i, base + 32 + i] = -1.0
            rotm[base + 32 + i, base + i] = 1.0
    rotT_sb = p.sb("e_rotT", [128, 128], F16)
    p.dma(rotT_sb[:], p.inline(rotm.T.astype(np.float16), "e_rotT_c").ap(), engine="gpsimd")
    ones2 = np.zeros((128, 2), np.float32)
    ones2[0:64, 0] = 1.0; ones2[64:128, 1] = 1.0
    ones2_sb = p.sb("e_ones2", [128, 2], F16)
    p.dma(ones2_sb[:], p.inline(ones2.astype(np.float16), "e_ones2_c").ap(), engine="gpsimd")
    sel2 = np.zeros((2, 128), np.float32)
    sel2[0, 0:64] = 1.0; sel2[1, 64:128] = 1.0
    sel2_sb = p.sb("e_sel2", [2, 128], F16)
    p.dma(sel2_sb[:], p.inline(sel2.astype(np.float16), "e_sel2_c").ap(), engine="gpsimd")
    idn16 = p.sb("idn_f16", [128, 128], F16)
    p.dma(idn16[:], p.inline(np.eye(128, dtype=np.float16), "idn_f16_c").ap(),
          engine="gpsimd")

    wq_sb = load_w_cast(p, "e_wq", wq, 8, 3 * D)
    lng_sb = p.sb("e_lng", [128, 8], F32); p.dma(lng_sb[:], io["lng_pack"])
    lnb_sb = p.sb("e_lnb", [128, 8], F32); p.dma(lnb_sb[:], io["lnb_pack"])
    b1a_sb = p.sb("e_b1a", [128, NHK1], F32); p.dma(b1a_sb[:], io["b1a_pack"])
    b1g_sb = p.sb("e_b1g", [128, NHK1], F32); p.dma(b1g_sb[:], io["b1g_pack"])
    b2_sb = p.sb("e_b2p", [128, 8], F32); p.dma(b2_sb[:], io["b2_pack"])
    qs_sb = p.sb("e_qsp", [128, 1], F32); p.dma(qs_sb[:], io["qs_pack"])
    ks_sb = p.sb("e_ksp", [128, 1], F32); p.dma(ks_sb[:], io["ks_pack"])

    halo = p.sb("e_halo", [128, 8 * 513], F32)
    cq = p.sb("e_cq", [128, T], F16)
    sq_ = p.sb("e_sq", [128, T], F16)
    ck = p.sb("e_ck", [128, T], F16)
    sk_ = p.sb("e_sk", [128, T], F16)
    for src, dst in [(io["t_csq"], cq), (io["t_ssq"], sq_),
                     (io["t_csk"], ck), (io["t_ssk"], sk_)]:
        p.dma(dst[:], src, engine="gpsimd")

    xq_bf = p.sb("e_xq", [128, 8 * TC], F16)
    qrot = p.sb("e_qrot", [128, 8 * TC], F16)
    krot = p.sb("e_krot", [128, 8 * TC], F16)
    kprev = p.sb("e_kprev", [128, 8 * 256], F16)
    vtok = p.sb("e_vtok", [128, 4 * D], F16)
    vprev = p.sb("e_vprev", [128, 2 * D], F16)
    qraw = p.sb("e_qraw", [128, TC], F16)
    sqb = p.sb("e_sqb", [128, TC], F16)
    nsb = p.sb("e_nsb", [2, TC], F32)
    rnorm = p.sb("e_rnorm", [2, TC], F16)
    t1b = p.sb("e_t1b", [128, TC], F16)
    t2b = p.sb("e_t2b", [128, TC], F16)
    Sb = p.sb("e_Sbuf", [128, 384], F32)
    m8n = p.sb("e_m8n", [128, 1], F32)
    m8b = p.sb("e_m8b", [128, 1], F16)
    mrow = p.sb("e_mrow", [1, 128], F16)
    PT = p.sb("e_PT", [128, 384], F16)
    na = p.sb("e_na", [64, 128], F32)
    rd2 = p.sb("e_rd2", [1, 128], F32)
    attnT = p.sb("e_attnT", [128, 8 * TC], F16)
    xtmp = p.sb("e_xtmp", [128, TC], F32)
    aa = p.sb("e_aa", [128, TC], F16)
    hg = p.sb("e_hg", [128, NHK1 * TC], F16)
    gg = p.sb("e_gg", [128, TC], F32)
    ev = p.sb("e_ev", [128, TC], F32)

    p.sb_alias("lnsc_xb8", "e_attnT", [128, 8 * TC], F16)
    p.sb_alias("lnsc_xq8", "e_qrot", [128, 8 * TC], F16)

    xsR = xs.rearrange("(c p) t -> p c t", p=128)
    wqv = rearr3(wq_sb, 8, 3 * D)

    def load_shifted(src_hbm_R, tc):
        for c in range(8):
            hsl = halo[:, c * 513:(c + 1) * 513]
            if tc == 0:
                p.memset(hsl[:, 0:1], 0.0)
                p.dma(hsl[:, 1:513], src_hbm_R[:, c, 0:TC])
            else:
                p.dma(hsl, src_hbm_R[:, c, tc * TC - 1:(tc + 1) * TC])
            off = 1 if c < 4 else 0
            p.vcopy(xq_bf[:, c * TC:(c + 1) * TC], hsl[:, off:off + TC])

    for tc in range(NTC):
        tsl = slice(tc * TC, (tc + 1) * TC)
        load_shifted(xsR, tc)
        for which, dst, ct, st_, scp in ((0, qrot, cq, sq_, qs_sb),
                                         (1, krot, ck, sk_, ks_sb)):
            for jc in range(8):
                for c in range(8):
                    p.mm(p.pp[0][:], wqv[:, c, which * D + jc * 128:
                                         which * D + (jc + 1) * 128],
                         xq_bf[:, c * TC:(c + 1) * TC],
                         start=(c == 0), stop=(c == 7))
                p.acopy(qraw[:], p.pp[0][:])
                p.act(sqb[:], p.pp[0][:], AF.Square)
                p.mm(p.pq[:], ones2_sb[:], sqb[:], start=True, stop=True)
                p.act(nsb[:], p.pq[:], AF.Sqrt)
                p.ts(nsb[:], nsb[:], p.const_col(1e-12)[0:2, :], None, ALU.max)
                p.recip(nsb[:], nsb[:])
                p.vcopy(rnorm[:], nsb[:])
                p.mm(p.pp[1][:], sel2_sb[:], rnorm[:], start=True, stop=True)
                p.tt(t1b[:], qraw[:], p.pp[1][:], ALU.mult)
                p.ts(t1b[:], t1b[:], scp[:], None, ALU.mult)
                p.mm(p.pp[2][:, 0:TC], rotT_sb[:], t1b[:], start=True, stop=True)
                rsl = dst[:, jc * TC:(jc + 1) * TC]
                p.tt(t2b[:], t1b[:], ct[:, tsl], ALU.mult)
                p.tt(qraw[:], p.pp[2][:, 0:TC], st_[:, tsl], ALU.mult)
                p.tt(rsl, t2b[:], qraw[:], ALU.add)
        for sub in range(4):
            for eh in range(2):
                for c in range(8):
                    p.mm(p.pp[0][:], xq_bf[:, c * TC + sub * 128:
                                           c * TC + sub * 128 + 128],
                         wqv[:, c, 2 * D + eh * 512:2 * D + (eh + 1) * 512],
                         start=(c == 0), stop=(c == 7))
                p.acopy(vtok[:, sub * D + eh * 512:sub * D + (eh + 1) * 512],
                        p.pp[0][:])
        for sub in range(4):
            qc = tc * 4 + sub
            for h in range(16):
                hp = 64 * (h % 2)
                hc = h // 2
                qsl = qrot[hp:hp + 64, hc * TC + sub * 128:hc * TC + sub * 128 + 128]

                def kblk(kb):
                    if kb >= tc * 4:
                        s = kb - tc * 4
                        return (krot[hp:hp + 64, hc * TC + s * 128:
                                     hc * TC + s * 128 + 128],
                                vtok[:, s * D + h * 64:s * D + (h + 1) * 64])
                    s = kb - (tc * 4 - 2)
                    return (kprev[hp:hp + 64, hc * 256 + s * 128:
                                  hc * 256 + s * 128 + 128],
                            vprev[:, s * D + h * 64:s * D + (h + 1) * 64])

                blocks = [b for b in (qc - 2, qc - 1, qc) if b >= 0]
                nb = len(blocks)
                boff = 3 - nb
                for bi, kb in enumerate(blocks):
                    ka, _ = kblk(kb)
                    p.mm(p.pp[bi][:, 0:128], qsl, ka, start=True, stop=True)
                    p.tt(Sb[:, (boff + bi) * 128:(boff + bi + 1) * 128],
                         p.pp[bi][:, 0:128],
                         mbias_sb[:, (boff + bi) * 128:(boff + bi + 1) * 128],
                         ALU.add)
                p.reduce(m8n[:], Sb[:, boff * 128:384], ALU.max, negate=True)
                p.vcopy(m8b[:], m8n[:])
                p.mm(p.pp[3][0:1, 0:128], m8b[:], idn16[:], start=True, stop=True)
                p.acopy(mrow[:], p.pp[3][0:1, 0:128])
                for bi, kb in enumerate(blocks):
                    ka, _ = kblk(kb)
                    p.mm(p.pp[bi][:, 0:128], ka, qsl, start=True, stop=False)
                    p.mm(p.pp[bi][:, 0:128], p.ones_r_f16[:], mrow[:],
                         start=False, stop=True)
                    p.act(PT[:, (boff + bi) * 128:(boff + bi + 1) * 128],
                          p.pp[bi][:, 0:128], AF.Exp, scale=8.0)
                p.tt(PT[:, boff * 128:384], PT[:, boff * 128:384],
                     m01T_sb[:, boff * 128:384], ALU.mult)
                for bi, kb in enumerate(blocks):
                    _, va = kblk(kb)
                    p.mm(p.pp[3][0:64, 0:128], va,
                         PT[:, (boff + bi) * 128:(boff + bi + 1) * 128],
                         start=(bi == 0), stop=(bi == nb - 1))
                for bi in range(nb):
                    p.mm(p.pr[0][0:1, 0:128], p.ones_c_f16[:],
                         PT[:, (boff + bi) * 128:(boff + bi + 1) * 128],
                         start=(bi == 0), stop=(bi == nb - 1))
                p.recip(rd2[:], p.pr[0][0:1, 0:128])
                p.acopy(na[:], p.pp[3][0:64, 0:128])
                p.mm(p.pp[0][0:64, 0:128], p.ones_r_f32[0:1, 0:64], rd2[:],
                     start=True, stop=True)
                asl = attnT[hp:hp + 64, hc * TC + sub * 128:hc * TC + sub * 128 + 128]
                p.tt(asl, na[:], p.pp[0][0:64, 0:128], ALU.mult)
        for c in range(8):
            p.vcopy(kprev[:, c * 256:(c + 1) * 256],
                    krot[:, c * TC + 256:(c + 1) * TC])
        p.vcopy(vprev[:, 0:D], vtok[:, 2 * D:3 * D])
        p.vcopy(vprev[:, D:2 * D], vtok[:, 3 * D:4 * D])
        for oc in range(8):
            wt = stream_w_colblock(p, "e_wob", wo, 8, oc * 128, 128)
            wov = rearr3(wt, 8, 128)
            for ec in range(8):
                p.mm(p.pp[0][:], wov[:, ec, :],
                     attnT[:, ec * TC:(ec + 1) * TC],
                     start=(ec == 0), stop=(ec == 7))
            p.acopy(ev[:], p.pp[0][:])
            p.tt(xtmp[:], ev[:], halo[:, oc * 513 + 1:oc * 513 + 513], ALU.add)
            p.dma(io["xmidT"][oc * 128:(oc + 1) * 128, tsl], xtmp[:])
        p.seq.barrier()

    xmR = io["xmidT"].rearrange("(c p) t -> p c t", p=128)
    for tc in range(NTC):
        tsl = slice(tc * TC, (tc + 1) * TC)
        load_shifted(xmR, tc)
        shf = [halo[:, c * 513 + (1 if c < 4 else 0):
                    c * 513 + (1 if c < 4 else 0) + TC] for c in range(8)]
        och = [xq_bf[:, c * TC:(c + 1) * TC] for c in range(8)]
        emit_ln(p, shf, och, TC, "eln", dt16=F16,
                g_ap=[lng_sb[:, c:c + 1] for c in range(8)],
                b_ap=[lnb_sb[:, c:c + 1] for c in range(8)])
        for k in range(NHK1):
            wcols = 128 if k < NHK1 - 1 else DIM_INNER - 128 * (NHK1 - 1)
            wt = stream_w_colblock(p, "e_w1a", w1, 8, k * 128, wcols,
                                   bufcols=128)
            wv = rearr3(wt, 8, 128)
            for c in range(8):
                p.mm(p.pp[0][0:wcols, :], wv[:, c, 0:wcols],
                     xq_bf[:, c * TC:(c + 1) * TC],
                     start=(c == 0), stop=(c == 7))
            p.ts(aa[0:wcols, :], p.pp[0][0:wcols, :],
                 b1a_sb[0:wcols, k:k + 1], None, ALU.add)
            wt2 = stream_w_colblock(p, "e_w1g", w1, 8, DIM_INNER + k * 128,
                                    wcols, bufcols=128)
            wv2 = rearr3(wt2, 8, 128)
            for c in range(8):
                p.mm(p.pp[1][0:wcols, :], wv2[:, c, 0:wcols],
                     xq_bf[:, c * TC:(c + 1) * TC],
                     start=(c == 0), stop=(c == 7))
            p.act(gg[0:wcols, :], p.pp[1][0:wcols, :], AF.Gelu,
                  bias=b1g_sb[0:wcols, k:k + 1])
            p.tt(hg[0:wcols, k * TC:(k + 1) * TC], aa[0:wcols, :],
                 gg[0:wcols, :], ALU.mult)
        w2R = w2.rearrange("(c p) m -> p c m", p=128)
        for oc in range(8):
            wt = p.sb("e_w2b", [128, NHK1 * 128], F16)
            p.dma(rearr3(wt, NHK1, 128), w2R[:, :, oc * 128:(oc + 1) * 128],
                  engine="gpsimd")
            wv = rearr3(wt, NHK1, 128)
            for k in range(NHK1):
                krows = 128 if k < NHK1 - 1 else DIM_INNER - 128 * (NHK1 - 1)
                p.mm(p.pp[0][:], wv[0:krows, k, :],
                     hg[0:krows, k * TC:(k + 1) * TC],
                     start=(k == 0), stop=(k == NHK1 - 1))
            p.ts(ev[:], p.pp[0][:], b2_sb[:, oc:oc + 1], None, ALU.add)
            p.tt(xtmp[:], ev[:], halo[:, oc * 513 + 1:oc * 513 + 513], ALU.add)
            p.dma(io["xoT"][oc * 128:(oc + 1) * 128, tsl], xtmp[:])
        p.seq.barrier()


def declare_E_io(p, suffix=""):
    return {
        "w_lqkv": p.din(f"w_lqkv{suffix}", [D, 3 * D], F32),
        "w_lo": p.din(f"w_lo{suffix}", [D, D], F32),
        "w_l1": p.din(f"w_l1{suffix}", [D, 2 * DIM_INNER], F32),
        "w_l2": p.din(f"w_l2{suffix}", [NHK1 * 128, D], F32),
        "qs_pack": p.din(f"qs_pack{suffix}", [128, 1], F32),
        "ks_pack": p.din(f"ks_pack{suffix}", [128, 1], F32),
        "lng_pack": p.din(f"lng_pack{suffix}", [128, 8], F32),
        "lnb_pack": p.din(f"lnb_pack{suffix}", [128, 8], F32),
        "b1a_pack": p.din(f"b1a_pack{suffix}", [128, NHK1], F32),
        "b1g_pack": p.din(f"b1g_pack{suffix}", [128, NHK1], F32),
        "b2_pack": p.din(f"b2_pack{suffix}", [128, 8], F32),
    }


def declare_E_tables(p):
    return {
        "t_csq": p.din("t_csq", [128, T], F32),
        "t_ssq": p.din("t_ssq", [128, T], F32),
        "t_csk": p.din("t_csk", [128, T], F32),
        "t_ssk": p.din("t_ssk", [128, T], F32),
    }


def build_launch_E(strict=False):
    p = Prog("l1ffn_layer", strict=strict)
    io = declare_E_io(p) | declare_E_tables(p)
    io["xsT"] = p.din("xsT", [D, T], F32)
    io["xmidT"] = p.dout("xmidT", [D, T], F32)
    io["xoT"] = p.dout("xoT", [D, T], F32)
    p.setup_consts()
    emit_E(p, io)
    return p


def rotary_tables_host():
    DIM_HEAD, WINDOW, n = 64, 256, T
    half = np.arange(0, DIM_HEAD, 2, dtype=np.float32)
    inv_freq = 1.0 / 10000.0 ** (half / DIM_HEAD)
    t = np.arange(n, dtype=np.float32)
    freqs = t[:, None] * inv_freq[None, :]
    freqs = np.concatenate([freqs, freqs], -1)
    base_scale = (half + 0.4 * DIM_HEAD) / (1.4 * DIM_HEAD)
    power = (t - n // 2) / (WINDOW // 2)
    scale = base_scale[None, :] ** power[:, None]
    scale = np.concatenate([scale, scale], -1)
    cos, sin = np.cos(freqs), np.sin(freqs)
    csq = (cos * scale).T; ssq = (sin * scale).T
    csk = (cos / scale).T; ssk = (sin / scale).T
    dbl = lambda a: np.ascontiguousarray(
        np.concatenate([a, a], 0).astype(np.float32))
    return dbl(csq), dbl(ssq), dbl(csk), dbl(ssk)


def prep_E_weights(inp, i):
    w = {}
    w["w_lqkv"] = np.ascontiguousarray(np.asarray(inp["L1_ffn_Wqkv"])[i])
    w["w_lo"] = np.ascontiguousarray(np.asarray(inp["L1_ffn_Wo"])[i])
    w["w_l1"] = np.ascontiguousarray(np.asarray(inp["L1_ffn_W1"])[i])
    w2 = np.asarray(inp["L1_ffn_W2"])[i]
    w2p = np.zeros((NHK1 * 128, D), np.float32)
    w2p[0:DIM_INNER] = w2
    w["w_l2"] = w2p
    qs = np.asarray(inp["L1_ffn_q_scale"])[i]
    ks = np.asarray(inp["L1_ffn_k_scale"])[i]
    w["qs_pack"] = np.ascontiguousarray(np.tile(qs, 2).reshape(128, 1))
    w["ks_pack"] = np.ascontiguousarray(np.tile(ks, 2).reshape(128, 1))
    w["lng_pack"] = _pack_cols(np.asarray(inp["L1_ffn_ln_g"])[i], 8)
    w["lnb_pack"] = _pack_cols(np.asarray(inp["L1_ffn_ln_b"])[i], 8)
    b1 = np.asarray(inp["L1_ffn_b1"])[i]
    pad = np.zeros(NHK1 * 128, np.float32)
    pad[0:DIM_INNER] = b1[0:DIM_INNER]
    w["b1a_pack"] = _pack_cols(pad, NHK1)
    pad2 = np.zeros(NHK1 * 128, np.float32)
    pad2[0:DIM_INNER] = b1[DIM_INNER:]
    w["b1g_pack"] = _pack_cols(pad2, NHK1)
    w["b2_pack"] = _pack_cols(np.asarray(inp["L1_ffn_b2"])[i], 8)
    return w


def prep_E_tables():
    csq, ssq, csk, ssk = rotary_tables_host()
    return {"t_csq": csq, "t_ssq": ssq, "t_csk": csk, "t_ssk": ssk}


def _pack_cols(vec, nchunks):
    v = np.asarray(vec, np.float32).reshape(nchunks, 128).T
    return np.ascontiguousarray(v)


def prep_A_weights(inp):
    w = {}
    w["w_inproj"] = np.ascontiguousarray(np.asarray(inp["L0_attn_in_proj_w"]).T)
    w["w_xproj"] = np.ascontiguousarray(np.asarray(inp["L0_attn_x_proj_w"]).T)
    w["w_dtproj"] = np.ascontiguousarray(np.asarray(inp["L0_attn_dt_proj_w"]).T)
    cwf = np.asarray(inp["L0_attn_conv_w"], np.float32).reshape(DI, 4)
    w["convw_pack"] = np.ascontiguousarray(
        cwf.reshape(16, 128, 4).transpose(1, 0, 2).reshape(128, 64))
    w["convb_pack"] = _pack_cols(inp["L0_attn_conv_b"], 16)
    w["dtb_pack"] = _pack_cols(inp["L0_attn_dt_proj_b"], 16)
    w["lng_pack"] = _pack_cols(inp["L0_attn_ln_g"], 8)
    w["lnb_pack"] = _pack_cols(inp["L0_attn_ln_b"], 8)
    return w


def emit_F(p, io):
    a_sb = p.sb("f_a", [128, 8 * TC], F32)
    b_sb = p.sb("f_b", [128, 8 * TC], F32)
    xsR = io["xsT"].rearrange("(c p) t -> p c t", p=128)
    x3R = io["x3T"].rearrange("(c p) t -> p c t", p=128)
    for tc in range(NTC):
        tsl = slice(tc * TC, (tc + 1) * TC)
        p.dma(rearr3(a_sb, 8, TC), xsR[:, :, tsl])
        p.dma(rearr3(b_sb, 8, TC), x3R[:, :, tsl])
        for c in range(8):
            p.tt(a_sb[:, c * TC:(c + 1) * TC], a_sb[:, c * TC:(c + 1) * TC],
                 b_sb[:, c * TC:(c + 1) * TC], ALU.add)
        xch = [a_sb[:, c * TC:(c + 1) * TC] for c in range(8)]
        emit_ln(p, xch, None, TC, "fln", out_chunks_f32=xch)
        for c in range(8):
            p.dma(io["x4T"][c * 128:(c + 1) * 128, tsl],
                  a_sb[:, c * TC:(c + 1) * TC])
        p.seq.barrier()


def build_launch_F(strict=False):
    p = Prog("final_ln", strict=strict)
    io = {"xsT": p.din("xsT", [D, T], F32), "x3T": p.din("x3T", [D, T], F32),
          "x4T": p.dout("x4T", [D, T], F32)}
    p.setup_consts()
    emit_F(p, io)
    return p


def build_fused(strict=False):
    p = Prog("megakernel", strict=strict)
    ioA = declare_A_io(p, ext=False)
    ioB = {
        "alog_pack": p.din("alog_pack", [128, 256], F32),
        "d_pack": p.din("d_pack", [128, 16], F32),
        "w_outproj": p.din("w_outproj", [DI, D], F32),
        "w_dimred": p.din("w_dimred", [2 * D, D], F32),
        "w_glu": p.din("w_glu", [D, 2 * D], F32),
        "drb_pack": p.din("drb_pack", [128, 8], F32),
        "glb_pack": p.din("glb_pack", [128, 16], F32),
        "xT": ioA["xT"],
        "y_f": p.dscratch("y_f", [D, T], BF16),
        "y_r": p.dscratch("y_r", [D, T], BF16),
        "x1T": p.dscratch("x1T", [D, T], F32),
    }
    ioC = {
        "x1T": ioB["x1T"],
        "w_gate": p.din("w_gate", [D, 8], F32),
        "w_exp1": p.din("w_exp1", [8, D, 1536], F32),
        "w_exp2": p.din("w_exp2", [8, 768, D], F32),
        "w_sh1": p.din("w_sh1", [D, 3072], F32),
        "w_sh2": p.din("w_sh2", [1536, D], F32),
        "w_scr": p.dscratch("w_scr", [8, TC], F32),
        "x2T": p.dscratch("x2T", [D, T], F32),
    }
    ioD = {
        "x2T": ioC["x2T"],
        "w_qkvT": p.din("w_qkvT", [D, 3 * D], F32),
        "qb_pack": p.din("qb_pack", [128, 16], F32),
        "qkvb_row": p.din("qkvb_row", [1, 3 * D], F32),
        "w_attnoT": p.din("w_attnoT", [D, D], F32),
        "ob_pack": p.din("ob_pack", [128, 8], F32),
        "x3T": p.dscratch("x3T", [D, T], F32),
    }
    tabs = declare_E_tables(p)
    ioEs = []
    xmid = p.dscratch("xmidT", [D, T], F32)
    xo_a = p.dscratch("xo_a", [D, T], F32)
    xo_b = p.dscratch("xo_b", [D, T], F32)
    for i in range(6):
        io = declare_E_io(p, suffix=f"_{i}") | tabs
        io["xsT"] = ioD["x3T"] if i == 0 else (xo_a if i % 2 == 1 else xo_b)
        io["xmidT"] = xmid
        io["xoT"] = xo_a if i % 2 == 0 else xo_b
        ioEs.append(io)
    ioF = {"xsT": ioEs[5]["xoT"], "x3T": ioD["x3T"],
           "x4T": p.dout("x4T", [D, T], F32)}

    p.setup_consts()
    p.make_arena()

    p.begin_stage("A")
    emit_A(p, ioA)
    p.begin_stage("B1")
    emit_B_common(p, ioB)
    emit_scan_pass(p, "f", {"delta": ioA["delta_f"], "u": ioA["u_f"],
                            "bc": ioA["bc_f"], "z": ioA["z_bf"],
                            "wop": ioB["w_outproj"], "y_dst": ioB["y_f"]})
    p.begin_stage("B2")
    emit_B_common(p, ioB)
    emit_scan_pass(p, "r", {"delta": ioA["delta_r"], "u": ioA["u_r"],
                            "bc": ioA["bc_r"], "z": ioA["z_bf"],
                            "wop": ioB["w_outproj"], "y_dst": ioB["y_r"]})
    p.begin_stage("B2p")
    emit_B2_post(p, ioB)
    p.begin_stage("C")
    emit_C(p, ioC)
    p.begin_stage("D")
    emit_D(p, ioD)
    for i in range(6):
        p.begin_stage(f"E{i}")
        emit_E(p, ioEs[i])
    p.begin_stage("F")
    emit_F(p, ioF)
    return p


def kernel(**inputs):
    inputs = {k: np.asarray(v) for k, v in inputs.items()}
    NB = inputs["L0_attn_x"].shape[0]
    xTs = [np.ascontiguousarray(inputs["L0_attn_x"][b].T) for b in range(NB)]

    base = {}
    base.update(prep_A_weights(inputs))
    base.update(prep_B_weights(inputs))
    base.update(prep_C_weights(inputs))
    base.update(prep_D_weights(inputs))
    base.update(prep_E_tables())
    for i in range(6):
        for k, v in prep_E_weights(inputs, i).items():
            base[f"{k}_{i}"] = v

    p = build_fused()
    in_maps = [dict(base, xT=xTs[b]) for b in range(NB)]
    res = p.run(in_maps)
    out = np.stack([np.asarray(res.results[b]["x4T"]).T for b in range(NB)],
                   axis=0)
    return np.ascontiguousarray(out, dtype=np.float32)


# version 05
# speedup vs baseline: 1.0584x; 1.0584x over previous
import contextlib
import numpy as np
import ml_dtypes

import concourse.bass as bass
import concourse.mybir as mybir
from concourse.bass_utils import run_bass_kernel_spmd

F32 = mybir.dt.float32
BF16 = mybir.dt.bfloat16
F16 = mybir.dt.float16
U8 = mybir.dt.uint8
AF = mybir.ActivationFunctionType
ALU = mybir.AluOpType
AX = mybir.AxisListType
BF = ml_dtypes.bfloat16

D = 1024
T = 2048
TC = 512
NTC = T // TC
DI = 2048
NST = 16
DTR = 64

ENGINES = ("sync", "tensor", "vector", "scalar", "gpsimd")


class Seq:

    def __init__(self, nc, strict=False):
        self.nc = nc
        self.strict = strict
        self.ops = []
        self.access = {}
        self.floor = {}
        self._last_barrier = None
        self.MAXACC = 48

    @staticmethod
    def _interval(ap):
        off = ap.offset
        span = 0
        for step, count in ap.ap:
            span += abs(step) * (count - 1)
        lo = off if all(s >= 0 for s, _ in ap.ap) else off - span
        return ap.tensor.name, lo, lo + span

    @staticmethod
    def _is_psum(ap):
        return type(ap.tensor).__name__ == "PSumTensorHandle"

    def _deps_for(self, ins, outs):
        deps = set()

        def scan(name, lo, hi, writing):
            fl = self.floor.get(name)
            if fl is not None:
                deps.add(fl)
            for idx, alo, ahi, wr in self.access.get(name, ()):
                if ahi < lo or alo > hi:
                    continue
                if wr or writing:
                    deps.add(idx)

        for ap in ins:
            name, lo, hi = self._interval(ap)
            scan(name, lo, hi, self._is_psum(ap))
        for ap in outs:
            name, lo, hi = self._interval(ap)
            scan(name, lo, hi, True)
        return deps

    def op(self, engine, emit_fn, ins=(), outs=(), dma=False):
        assert engine in ENGINES, engine
        i = len(self.ops)
        if self.strict:
            deps = {i - 1} if i else set()
        else:
            deps = self._deps_for(ins, outs)
            if self._last_barrier is not None:
                deps.add(self._last_barrier)
        self.ops.append((engine, emit_fn, dma, deps))
        for ap in ins:
            name, lo, hi = self._interval(ap)
            self.access.setdefault(name, []).append((i, lo, hi, self._is_psum(ap)))
        for ap in outs:
            name, lo, hi = self._interval(ap)
            self.access.setdefault(name, []).append((i, lo, hi, True))
        for name, lst in list(self.access.items()):
            if len(lst) > self.MAXACC:
                drop = lst[: len(lst) - self.MAXACC]
                self.floor[name] = max(
                    max(d[0] for d in drop), self.floor.get(name, -1)
                )
                self.access[name] = lst[len(lst) - self.MAXACC:]
        return i

    def barrier(self):
        i = len(self.ops) - 1
        if i < 0:
            return
        for name in list(self.access):
            self.floor[name] = i
            self.access[name] = []
        self._last_barrier = i

    def emit(self, block, sems):
        nops = len(self.ops)
        inc = {e: [0] * (nops + 1) for e in ENGINES}
        for i, (e, _f, dma, _d) in enumerate(self.ops):
            for e2 in ENGINES:
                inc[e2][i + 1] = inc[e2][i] + ((16 if dma else 1) if e2 == e else 0)
        gf = getattr(self, "_global_floor", None)

        def make_body(eng_name):
            def body(eng):
                last_waited = dict.fromkeys(ENGINES, 0)
                for i, (e, emit_fn, dma, deps) in enumerate(self.ops):
                    if e != eng_name:
                        continue
                    need = dict.fromkeys(ENGINES, 0)
                    dd = set(deps)
                    if gf is not None and i > gf:
                        dd.add(gf)
                    for d in dd:
                        for e2 in ENGINES:
                            v = inc[e2][d + 1]
                            if v > need[e2]:
                                need[e2] = v
                    for e2 in ENGINES:
                        if e2 == eng_name and e2 == "tensor":
                            continue
                        if need[e2] > last_waited[e2]:
                            eng.wait_ge(sems[e2], need[e2])
                            last_waited[e2] = need[e2]
                    inst = emit_fn(eng)
                    assert inst is not None, f"op {i} returned None"
                    inst.then_inc(sems[e], 16 if dma else 1)
                if eng_name == "sync":
                    for e2 in ENGINES:
                        tot = inc[e2][nops]
                        if tot > last_waited[e2]:
                            eng.wait_ge(sems[e2], tot)
            return body

        block.sync(make_body("sync"))
        block.tensor(make_body("tensor"))
        block.vector(make_body("vector"))
        block.scalar(make_body("scalar"))
        block.gpsimd(make_body("gpsimd"))


class Prog:

    def __init__(self, name, strict=False):
        self.nc = bass.Bass("TRN2", target_bir_lowering=False, name=name)
        self.seq = Seq(self.nc, strict=strict)
        self.inputs = {}
        self.outputs = {}
        self._const_cols = {}
        self._setup_done = False
        self._sb_cache = {}
        self._persist = set()
        self._arena = None
        self._bump = 0
        self._stage = "G"

    def din(self, name, shape, dt=F32):
        h = self.nc.dram_tensor(name, list(shape), dt, kind="ExternalInput")
        self.inputs[name] = h
        return h.ap()

    def dout(self, name, shape, dt=F32):
        h = self.nc.dram_tensor(name, list(shape), dt, kind="ExternalOutput")
        self.outputs[name] = h
        return h.ap()

    def dscratch(self, name, shape, dt=F32):
        return self.nc.dram_tensor(name, list(shape), dt, kind="Internal").ap()

    def make_arena(self, reserve=6 * 1024):
        nbytes = self.nc.sbuf_bytes_remaining - reserve
        h = self.nc.alloc_sbuf_tensor("stage_arena", [128, nbytes], U8)
        base = self.nc.lookup_mloc(h).addr
        self._arena = (base, nbytes)

    def begin_stage(self, tag):
        self.seq.barrier()
        self._stage = tag
        self._bump = 0
        self._sb_cache = {k: v for k, v in self._sb_cache.items()
                          if k in self._persist}

    def sb(self, name, shape, dt=F32, persist=False):
        if name in self._sb_cache:
            t, sh, d = self._sb_cache[name]
            assert list(sh) == list(shape) and d == dt, f"sb {name} shape clash"
            return t
        if persist or self._arena is None:
            t = self.nc.alloc_sbuf_tensor(name, list(shape), dt)
            if persist:
                self._persist.add(name)
        else:
            per_part = int(np.prod(shape[1:])) * mybir.dt.size(dt)
            off = (self._bump + 31) & ~31
            base, nbytes = self._arena
            assert off + per_part <= nbytes, \
                f"arena overflow: {self._stage}/{name} needs {per_part} at {off} (cap {nbytes})"
            t = self.nc.alloc_sbuf_tensor_at(
                f"{self._stage}_{name}", list(shape), dt, offset=base + off)
            self._bump = off + per_part
        self._sb_cache[name] = (t, list(shape), dt)
        return t

    def ps(self, name, shape, dt=F32):
        return self.nc.alloc_psum_tensor(name, list(shape), dt)

    def sb_alias(self, name, existing, shape, dt):
        t, sh, d = self._sb_cache[existing]
        assert d == dt and sh[0] >= shape[0] and \
            int(np.prod(sh[1:])) >= int(np.prod(shape[1:])), (name, sh, shape)
        self._sb_cache[name] = (t, shape, dt)

    def inline(self, arr, name):
        if not hasattr(self, "_inline_cache"):
            self._inline_cache = {}
        if name not in self._inline_cache:
            self._inline_cache[name] = self.nc.inline_tensor(
                np.ascontiguousarray(arr), name)
        return self._inline_cache[name]

    def setup_consts(self):
        assert not self._setup_done
        self._setup_done = True
        self.ones_r_f32 = self.sb("ones_r_f32", [1, 128], F32, persist=True)
        self.ones_r_bf = self.sb("ones_r_bf", [1, 128], BF16, persist=True)
        self.ones_c_f32 = self.sb("ones_c_f32", [128, 1], F32, persist=True)
        self.ones_c_bf = self.sb("ones_c_bf", [128, 1], BF16, persist=True)
        self.ones_r_f16 = self.sb("ones_r_f16", [1, 128], F16, persist=True)
        self.ones_c_f16 = self.sb("ones_c_f16", [128, 1], F16, persist=True)
        for t in (self.ones_r_f32, self.ones_r_bf, self.ones_c_f32,
                  self.ones_c_bf, self.ones_r_f16, self.ones_c_f16):
            self.v(lambda e, t=t: e.memset(t[:], 1.0), outs=[t[:]])
        self.pp = [self.ps(f"pp{i}", [128, 512], F32) for i in range(4)]
        self.pr = [self.ps(f"pr{i}", [1, 512], F32) for i in range(2)]
        self.pq = self.ps("pq", [2, TC], F32)

    def const_col(self, value):
        if value not in self._const_cols:
            t = self.sb(f"cc_{len(self._const_cols)}", [128, 1], F32,
                        persist=True)
            self.v(lambda e: e.memset(t[:], float(value)), outs=[t[:]])
            self._const_cols[value] = t
        return self._const_cols[value][:]

    def dma(self, out, in_, engine="sync"):
        self.seq.op(engine, lambda e: e.dma_start(out=out, in_=in_),
                    ins=[in_], outs=[out], dma=True)

    def mm(self, out, lhsT, rhs, start=True, stop=True):
        self.seq.op("tensor",
                    lambda e: e.matmul(out, lhsT, rhs, start=start, stop=stop),
                    ins=[lhsT, rhs], outs=[out])

    def trans(self, out, in_, identity):
        self.seq.op("tensor", lambda e: e.transpose(out, in_, identity),
                    ins=[in_, identity], outs=[out])

    def act(self, out, in_, func, bias=0.0, scale=1.0, accum_out=None):
        ins = [in_]
        if not isinstance(bias, float):
            ins.append(bias)
        if not isinstance(scale, float):
            ins.append(scale)
        outs = [out] + ([accum_out] if accum_out is not None else [])
        if accum_out is not None:
            self.seq.op("scalar", lambda e: e.activation(
                out, in_, func, bias=bias, scale=scale, accum_out=accum_out),
                ins=ins, outs=outs)
        else:
            self.seq.op("scalar", lambda e: e.activation(
                out, in_, func, bias=bias, scale=scale), ins=ins, outs=outs)

    def acopy(self, out, in_):
        self.seq.op("scalar", lambda e: e.copy(out, in_), ins=[in_], outs=[out])

    def v(self, emit, ins=(), outs=()):
        self.seq.op("vector", emit, ins=ins, outs=outs)

    def g(self, emit, ins=(), outs=()):
        self.seq.op("gpsimd", emit, ins=ins, outs=outs)

    def tt(self, out, a, b, op, engine="vector"):
        self.seq.op(engine, lambda e: e.tensor_tensor(out, a, b, op),
                    ins=[a, b], outs=[out])

    def ts(self, out, in0, s1, s2, op0, op1=None, engine="vector"):
        ins = [in0] + [x for x in (s1, s2) if x is not None and not isinstance(x, float)]
        if op1 is None:
            self.seq.op(engine, lambda e: e.tensor_scalar(out, in0, s1, s2, op0),
                        ins=ins, outs=[out])
        else:
            self.seq.op(engine, lambda e: e.tensor_scalar(out, in0, s1, s2, op0, op1),
                        ins=ins, outs=[out])

    def stt(self, out, in0, scalar, in1, op0, op1, engine="vector"):
        ins = [in0, in1] + ([scalar] if not isinstance(scalar, float) else [])
        self.seq.op(engine, lambda e: e.scalar_tensor_tensor(
            out, in0, scalar, in1, op0, op1), ins=ins, outs=[out])

    def tts(self, out, d0, d1, init, op0, op1):
        ins = [d0, d1] + ([init] if not isinstance(init, float) else [])
        self.seq.op("vector", lambda e: e.tensor_tensor_scan(
            out, d0, d1, init, op0, op1), ins=ins, outs=[out])

    def vcopy(self, out, in_):
        self.v(lambda e: e.tensor_copy(out, in_), ins=[in_], outs=[out])

    def reduce(self, out, in_, op, negate=False):
        self.seq.op("vector", lambda e: e.tensor_reduce(
            out, in_, AX.X, op, negate=negate), ins=[in_], outs=[out])

    def recip(self, out, in_):
        self.seq.op("vector", lambda e: e.reciprocal(out, in_),
                    ins=[in_], outs=[out])

    def memset(self, ap, val, engine="vector"):
        self.seq.op(engine, lambda e: e.memset(ap, val), ins=[], outs=[ap])

    def run(self, in_maps, trace=False):
        with contextlib.ExitStack() as st:
            sems = {e: st.enter_context(self.nc.semaphore(f"s_{e}"))
                    for e in ENGINES}
            blk = st.enter_context(self.nc.Block())
            self.seq.emit(blk, sems)
        return run_bass_kernel_spmd(self.nc, in_maps,
                                    core_ids=list(range(len(in_maps))),
                                    trace=trace)


def rearr3(sb_handle, c, inner):
    return sb_handle[:].rearrange("p (c m) -> p c m", c=c)


def load_w_cast(p, name, hbm_ap, kchunks, mcols, dt=BF16):
    t = p.sb(name, [128, kchunks * mcols], dt)
    src = hbm_ap.rearrange("(c p) m -> p c m", p=128)
    dst = rearr3(t, kchunks, mcols)
    p.dma(dst, src, engine="gpsimd")
    return t


def emit_ln(p, x_chunks_f32, out_chunks_bf, tc_len, pref, g_ap=None, b_ap=None,
            out_chunks_f32=None, nchunks=8, dm=None, dt16=BF16):
    dm = dm or (128 * nchunks)
    assert tc_len <= 512
    sums = p.pr[0][0:1, 0:tc_len]
    sqs = p.pr[1][0:1, 0:tc_len]
    xb = p.sb(f"lnsc_xb{nchunks}", [128, nchunks * tc_len], dt16)
    xq = p.sb(f"lnsc_xq{nchunks}", [128, nchunks * tc_len], dt16)
    ones_c16 = p.ones_c_bf if dt16 == BF16 else p.ones_c_f16
    for c in range(nchunks):
        xbc = xb[:, c * tc_len:(c + 1) * tc_len]
        p.vcopy(xbc, x_chunks_f32[c])
        p.act(xq[:, c * tc_len:(c + 1) * tc_len], x_chunks_f32[c], AF.Square)
    for c in range(nchunks):
        p.mm(sums, ones_c16[:], xb[:, c * tc_len:(c + 1) * tc_len],
             start=(c == 0), stop=(c == nchunks - 1))
    for c in range(nchunks):
        p.mm(sqs, ones_c16[:], xq[:, c * tc_len:(c + 1) * tc_len],
             start=(c == 0), stop=(c == nchunks - 1))
    mu = p.sb("lnsc_mu", [1, 512], F32)[0:1, 0:tc_len]
    rstd = p.sb("lnsc_rs", [1, 512], F32)[0:1, 0:tc_len]
    var = p.sb("lnsc_va", [1, 512], F32)[0:1, 0:tc_len]
    mu2 = p.sb("lnsc_m2", [1, 512], F32)[0:1, 0:tc_len]
    p.ts(mu, sums, 1.0 / dm, None, ALU.mult)
    p.ts(var, sqs, 1.0 / dm, None, ALU.mult)
    p.act(mu2, mu, AF.Square)
    p.tt(var, var, mu2, ALU.subtract)
    p.act(var, var, AF.Sqrt, bias=p.const_col(1e-5)[0:1, :])
    p.recip(rstd, var)
    mub = p.pp[2][:, 0:tc_len]
    rsb = p.pp[3][:, 0:tc_len]
    p.mm(mub, p.ones_r_f32[:], mu)
    p.mm(rsb, p.ones_r_f32[:], rstd)
    tmp = p.sb("lnsc_t", [128, 512], F32)[:, 0:tc_len]
    for c in range(nchunks):
        p.tt(tmp, x_chunks_f32[c], mub, ALU.subtract)
        dst_f32 = out_chunks_f32[c] if out_chunks_f32 is not None else tmp
        p.tt(dst_f32, tmp, rsb, ALU.mult)
        if g_ap is not None:
            p.ts(dst_f32, dst_f32, g_ap[c], b_ap[c], ALU.mult, ALU.add)
        if out_chunks_bf is not None:
            p.vcopy(out_chunks_bf[c], dst_f32)


def declare_A_io(p, ext=True):
    io = {
        "xT": p.din("xT", [D, T], F32),
        "w_inproj": p.din("w_inproj", [D, 2 * DI], F32),
        "w_xproj": p.din("w_xproj", [DI, 96], F32),
        "w_dtproj": p.din("w_dtproj", [DTR, DI], F32),
        "convw_pack": p.din("convw_pack", [128, 16 * 4], F32),
        "convb_pack": p.din("convb_pack", [128, 16], F32),
        "dtb_pack": p.din("dtb_pack", [128, 16], F32),
        "lng_pack": p.din("lng_pack", [128, 8], F32),
        "lnb_pack": p.din("lnb_pack", [128, 8], F32),
    }
    mk_out = p.dout if ext else p.dscratch
    io["xi_pre"] = mk_out("xi_pre", [DI, T], BF16)
    io["z_bf"] = mk_out("z_bf", [DI, T], BF16)
    for d in "fr":
        io[f"u_{d}"] = mk_out(f"u_{d}", [DI, T], BF16)
        io[f"delta_{d}"] = mk_out(f"delta_{d}", [DI, T], F32)
        io[f"bc_{d}"] = mk_out(f"bc_{d}", [32, T], F32)
    return io


def emit_A(p, io):
    xT = io["xT"]
    wip_sb = load_w_cast(p, "wip_sb", io["w_inproj"], 8, 2 * DI)
    lng_sb = p.sb("lng_sb", [128, 8], F32); p.dma(lng_sb[:], io["lng_pack"])
    lnb_sb = p.sb("lnb_sb", [128, 8], F32); p.dma(lnb_sb[:], io["lnb_pack"])

    xTr = xT.rearrange("(c p) t -> p c t", p=128)

    for tc in range(NTC):
        tsl = slice(tc * TC, (tc + 1) * TC)
        x_sb = p.sb(f"x_sb{tc % 2}", [128, 8 * TC], F32)
        xn_bf = p.sb(f"xn_bf{tc % 2}", [128, 8 * TC], BF16)
        p.dma(rearr3(x_sb, 8, TC), xTr[:, :, tsl])
        xch = [x_sb[:, c * TC:(c + 1) * TC] for c in range(8)]
        och = [xn_bf[:, c * TC:(c + 1) * TC] for c in range(8)]
        emit_ln(p, xch, och, TC, f"ln{tc}",
                g_ap=[lng_sb[:, c:c + 1] for c in range(8)],
                b_ap=[lnb_sb[:, c:c + 1] for c in range(8)])
        wv = rearr3(wip_sb, 8, 2 * DI)
        for m in range(32):
            mm_ps = p.pp[m % 2]
            ev_bf = p.sb(f"ev_bf{m % 2}", [128, TC], BF16)
            for c in range(8):
                p.mm(mm_ps[:], wv[:, c, m * 128:(m + 1) * 128],
                     xn_bf[:, c * TC:(c + 1) * TC],
                     start=(c == 0), stop=(c == 7))
            p.acopy(ev_bf[:], mm_ps[:])
            dst = io["xi_pre"] if m < 16 else io["z_bf"]
            mm_loc = m if m < 16 else m - 16
            p.dma(dst[mm_loc * 128:(mm_loc + 1) * 128, tsl], ev_bf[:])

    p.begin_stage("A2")
    wxp_sb = load_w_cast(p, "wxp_sb", io["w_xproj"], 16, 96)
    wdt_sb = p.sb("wdt_sb", [64, DI], BF16)
    p.dma(wdt_sb[:], io["w_dtproj"], engine="gpsimd")
    cw_sb = p.sb("cw_sb", [128, 64], F32); p.dma(cw_sb[:], io["convw_pack"])
    cb_sb = p.sb("cb_sb", [128, 16], F32); p.dma(cb_sb[:], io["convb_pack"])
    dtb_sb = p.sb("dtb_sb", [128, 16], F32); p.dma(dtb_sb[:], io["dtb_pack"])
    bc_sb = p.sb("bc_sb", [32, TC], F32)

    for di, d in enumerate("fr"):
        for tc in range(NTC):
            tsl = slice(tc * TC, (tc + 1) * TC)
            u_par = (di * NTC + tc) % 2
            u_sb = p.sb(f"u_sb{u_par}", [128, 16 * TC], BF16)
            for c in range(16):
                rows = slice(c * 128, (c + 1) * 128)
                halo = p.sb(f"halo{c % 2}", [128, TC + 3], BF16)
                acc = p.sb(f"acc{c % 2}", [128, TC], F32)
                if d == "f":
                    lo = tc * TC - 3
                    if lo < 0:
                        p.memset(halo[:, 0:3], 0.0)
                        p.dma(halo[:, 3:TC + 3], io["xi_pre"][rows, 0:TC])
                    else:
                        p.dma(halo[:], io["xi_pre"][rows, lo:tc * TC + TC])
                    offs = [(3, 3), (2, 2), (1, 1), (0, 0)]
                else:
                    hi = tc * TC + TC + 3
                    if hi > T:
                        p.dma(halo[:, 0:TC], io["xi_pre"][rows, tsl])
                        p.memset(halo[:, TC:TC + 3], 0.0)
                    else:
                        p.dma(halo[:], io["xi_pre"][rows, tc * TC:hi])
                    offs = [(3, 0), (2, 1), (1, 2), (0, 3)]
                (k0, o0) = offs[0]
                p.ts(acc[:], halo[:, o0:o0 + TC], cw_sb[:, c * 4 + k0:c * 4 + k0 + 1],
                     cb_sb[:, c:c + 1], ALU.mult, ALU.add)
                for (k, o) in offs[1:]:
                    p.stt(acc[:], halo[:, o:o + TC], cw_sb[:, c * 4 + k:c * 4 + k + 1],
                          acc[:], ALU.mult, ALU.add)
                usl = u_sb[:, c * TC:(c + 1) * TC]
                p.act(usl, acc[:], AF.Silu)
                p.dma(io[f"u_{d}"][rows, tsl], usl)
            dbc_ps = p.pp[2 + (tc % 2)][0:96, :]
            dt_bf = p.sb(f"dt_bf{tc % 2}", [64, TC], BF16)
            wxv = rearr3(wxp_sb, 16, 96)
            for c in range(16):
                p.mm(dbc_ps[:], wxv[:, c, :], u_sb[:, c * TC:(c + 1) * TC],
                     start=(c == 0), stop=(c == 15))
            p.acopy(dt_bf[:], dbc_ps[0:64, :])
            p.acopy(bc_sb[:], dbc_ps[64:96, :])
            p.dma(io[f"bc_{d}"][:, tsl], bc_sb[:])
            for m in range(16):
                mm_ps = p.pp[m % 2]
                sp1 = p.sb(f"sp1_{m % 2}", [128, TC], F32)
                sp2 = p.sb(f"sp2_{m % 2}", [128, TC], F32)
                delta_sb = p.sb(f"delta_sb{m % 2}", [128, TC], F32)
                p.mm(mm_ps[:], wdt_sb[:, m * 128:(m + 1) * 128], dt_bf[:])
                b_ap = dtb_sb[:, m:m + 1]
                p.act(sp1[:], mm_ps[:], AF.Abs, bias=b_ap)
                p.act(sp2[:], sp1[:], AF.Exp, scale=-1.0)
                p.act(sp1[:], sp2[:], AF.Ln, bias=1.0)
                p.ts(sp2[:], mm_ps[:], b_ap, 0.0, ALU.add, ALU.max)
                p.tt(delta_sb[:], sp2[:], sp1[:], ALU.add)
                p.dma(io[f"delta_{d}"][m * 128:(m + 1) * 128, tsl], delta_sb[:])


def build_launch_A(strict=False):
    p = Prog("mamba_pre", strict=strict)
    io = declare_A_io(p)
    p.setup_consts()
    emit_A(p, io)
    return p


def stream_w_colblock(p, name, hbm_ap, kchunks, col0, ncols, dt=BF16,
                      bufcols=None, par=None):
    bufcols = bufcols or ncols
    eng = "gpsimd"
    if par is not None:
        name = f"{name}_p{par & 1}"
    t = p.sb(name, [128, kchunks * bufcols], dt)
    src = hbm_ap.rearrange("(c p) m -> p c m", p=128)[:, :, col0:col0 + ncols]
    dst = rearr3(t, kchunks, bufcols)[:, :, 0:ncols]
    p.dma(dst, src, engine=eng)
    return t


def emit_scan_pass(p, d, aps):
    rev = (d == "r")
    delta_sb = p.sb("sc_delta", [128, 16 * TC], F32)
    u_sb = p.sb("sc_u", [128, 16 * TC], BF16)
    du_sb = p.sb("sc_du", [128, 16 * TC], BF16)
    y_sb = p.sb("sc_y", [128, 16 * TC], F32)
    ys_bf = p.sb("sc_ys", [128, 16 * TC], BF16)
    bcs = p.sb("sc_bcs", [1, 32 * TC], BF16)
    a_sb = p.sb("sc_a", [128, TC], F32)
    b_sb = p.sb("sc_b", [128, TC], F32)
    h_sb = p.sb("sc_h", [128, TC], F32)
    g_sb = p.sb("sc_g", [128, TC], F32)
    zc = p.sb("sc_zc", [128, TC], BF16)
    sz = p.sb("sc_sz", [128, TC], F32)
    hst = p.sb("sc_hst", [128, 256], F32)
    p.memset(hst[:], 0.0)

    deltaR = aps["delta"].rearrange("(c p) t -> p c t", p=128)
    uR = aps["u"].rearrange("(c p) t -> p c t", p=128)
    zR = aps["z"].rearrange("(c p) t -> p c t", p=128)
    tcs = list(range(NTC - 1, -1, -1)) if rev else list(range(NTC))

    for tc in tcs:
        tsl = slice(tc * TC, (tc + 1) * TC)
        p.dma(rearr3(delta_sb, 16, TC), deltaR[:, :, tsl])
        p.dma(rearr3(u_sb, 16, TC), uR[:, :, tsl])
        bcsrc = aps["bc"][:, tsl].unsqueeze(0)
        p.dma(bcs[:].rearrange("p (r t) -> p r t", r=32), bcsrc,
              engine="gpsimd")
        for dc in range(16):
            csl = slice(dc * TC, (dc + 1) * TC)
            p.tt(du_sb[:, csl], delta_sb[:, csl], u_sb[:, csl], ALU.mult)
        for n in range(NST):
            p.mm(p.pp[2][:], p.ones_r_bf[:], bcs[0:1, n * TC:(n + 1) * TC])
            p.mm(p.pp[3][:], p.ones_r_bf[:], bcs[0:1, (16 + n) * TC:(17 + n) * TC])
            for dc in range(16):
                csl = slice(dc * TC, (dc + 1) * TC)
                p.act(a_sb[:], delta_sb[:, csl], AF.Exp,
                      scale=p.A_pack_sb[:, dc * NST + n:dc * NST + n + 1])
                p.tt(b_sb[:], du_sb[:, csl], p.pp[2][:], ALU.mult)
                if rev:
                    p.tts(h_sb[:, ::-1], a_sb[:, ::-1], b_sb[:, ::-1],
                          hst[:, dc * NST + n:dc * NST + n + 1], ALU.mult, ALU.add)
                    p.vcopy(hst[:, dc * NST + n:dc * NST + n + 1], h_sb[:, 0:1])
                else:
                    p.tts(h_sb[:], a_sb[:], b_sb[:],
                          hst[:, dc * NST + n:dc * NST + n + 1], ALU.mult, ALU.add)
                    p.vcopy(hst[:, dc * NST + n:dc * NST + n + 1],
                            h_sb[:, TC - 1:TC])
                p.tt(g_sb[:], h_sb[:], p.pp[3][:], ALU.mult)
                if n == 0:
                    p.vcopy(y_sb[:, csl], g_sb[:])
                else:
                    p.tt(y_sb[:, csl], y_sb[:, csl], g_sb[:], ALU.add)
        for dc in range(16):
            csl = slice(dc * TC, (dc + 1) * TC)
            p.stt(y_sb[:, csl], u_sb[:, csl], p.D_pack_sb[:, dc:dc + 1],
                  y_sb[:, csl], ALU.mult, ALU.add)
            p.dma(zc[:], zR[:, dc, tsl])
            p.act(sz[:], zc[:], AF.Silu)
            p.tt(ys_bf[:, csl], y_sb[:, csl], sz[:], ALU.mult)
        for oc in range(8):
            wt = stream_w_colblock(p, "sc_wop", aps["wop"], 16, oc * 128, 128,
                                   par=oc)
            wv = rearr3(wt, 16, 128)
            evb = p.sb(f"sc_ev{oc % 2}", [128, TC], BF16)
            for dc in range(16):
                p.mm(p.pp[oc % 2][:], wv[:, dc, :], ys_bf[:, dc * TC:(dc + 1) * TC],
                     start=(dc == 0), stop=(dc == 15))
            p.acopy(evb[:], p.pp[oc % 2][:])
            p.dma(aps["y_dst"][oc * 128:(oc + 1) * 128, tsl], evb[:])


def emit_B_common(p, io):
    t = p.sb("alog_sb", [128, 256], F32)
    p.dma(t[:], io["alog_pack"])
    p.A_pack_sb = p.sb("A_pack_sb", [128, 256], F32)
    p.act(p.A_pack_sb[:], t[:], AF.Exp)
    p.ts(p.A_pack_sb[:], p.A_pack_sb[:], -1.0, None, ALU.mult)
    p.D_pack_sb = p.sb("D_pack_sb", [128, 16], F32)
    p.dma(p.D_pack_sb[:], io["d_pack"])


def emit_B2_post(p, io):
    drb_sb = p.sb("drb_sb", [128, 8], F32); p.dma(drb_sb[:], io["drb_pack"])
    glb_sb = p.sb("glb_sb", [128, 16], F32); p.dma(glb_sb[:], io["glb_pack"])
    yf_sb = p.sb("sc_u", [128, 16 * TC], BF16)[:, 0:8 * TC]
    yr_sb = p.sb("sc_u", [128, 16 * TC], BF16)[:, 8 * TC:16 * TC]
    gl_bf = p.sb("sc_du", [128, 16 * TC], BF16)[:, 0:8 * TC]
    ga_bf = p.sb("sc_du", [128, 16 * TC], BF16)[:, 8 * TC:16 * TC]
    sg = p.sb("sc_a", [128, TC], F32)[:]
    mo = p.sb("sc_y", [128, 16 * TC], F32)[:, 0:8 * TC]
    x_sb = p.sb("sc_y", [128, 16 * TC], F32)[:, 8 * TC:16 * TC]
    x1_sb = p.sb("sc_delta", [128, 16 * TC], F32)[:, 0:8 * TC]
    yfR = io["y_f"].rearrange("(c p) t -> p c t", p=128)
    yrR = io["y_r"].rearrange("(c p) t -> p c t", p=128)
    xTr = io["xT"].rearrange("(c p) t -> p c t", p=128)
    wdr, wglu = io["w_dimred"], io["w_glu"]
    for tc in range(NTC):
        tsl = slice(tc * TC, (tc + 1) * TC)
        p.dma(rearr3(yf_sb, 8, TC), yfR[:, :, tsl])
        p.dma(rearr3(yr_sb, 8, TC), yrR[:, :, tsl])
        p.dma(rearr3(x_sb, 8, TC), xTr[:, :, tsl])
        for oc in range(8):
            wt = stream_w_colblock(p, "po_wdr", wdr, 16, oc * 128, 128, par=oc)
            wv = rearr3(wt, 16, 128)
            for c in range(8):
                p.mm(p.pp[oc % 2][:], wv[:, c, :], yf_sb[:, c * TC:(c + 1) * TC],
                     start=(c == 0), stop=False)
            for c in range(8):
                p.mm(p.pp[oc % 2][:], wv[:, 8 + c, :], yr_sb[:, c * TC:(c + 1) * TC],
                     start=False, stop=(c == 7))
            p.act(gl_bf[:, oc * TC:(oc + 1) * TC], p.pp[oc % 2][:], AF.Gelu,
                  bias=drb_sb[:, oc:oc + 1])
        for m in range(16):
            wt = stream_w_colblock(p, "po_wglu", wglu, 8, m * 128, 128, par=m)
            wv = rearr3(wt, 8, 128)
            for c in range(8):
                p.mm(p.pp[m % 2][:], wv[:, c, :], gl_bf[:, c * TC:(c + 1) * TC],
                     start=(c == 0), stop=(c == 7))
            if m < 8:
                p.ts(ga_bf[:, m * TC:(m + 1) * TC], p.pp[m % 2][:],
                     glb_sb[:, m:m + 1], None, ALU.add)
            else:
                mm2 = m - 8
                sg2 = p.sb(f"po_sg{m % 2}", [128, TC], F32)
                p.act(sg2[:], p.pp[m % 2][:], AF.Sigmoid, bias=glb_sb[:, m:m + 1])
                csl = slice(mm2 * TC, (mm2 + 1) * TC)
                p.tt(mo[:, csl], ga_bf[:, csl], sg2[:], ALU.mult)
                p.stt(mo[:, csl], x_sb[:, csl], 2.0, mo[:, csl], ALU.mult, ALU.add)
        xch = [mo[:, c * TC:(c + 1) * TC] for c in range(8)]
        o32 = [x1_sb[:, c * TC:(c + 1) * TC] for c in range(8)]
        emit_ln(p, xch, None, TC, "pln", out_chunks_f32=o32)
        for c in range(8):
            p.dma(io["x1T"][c * 128:(c + 1) * 128, tsl],
                  x1_sb[:, c * TC:(c + 1) * TC])


def build_launch_B1(strict=False):
    p = Prog("scan_f", strict=strict)
    aps = {
        "delta": p.din("delta_f", [DI, T], F32),
        "u": p.din("u_f", [DI, T], BF16),
        "bc": p.din("bc_f", [32, T], F32),
        "z": p.din("z_bf", [DI, T], BF16),
        "wop": p.din("w_outproj", [DI, D], F32),
        "y_dst": p.dout("y_f", [D, T], BF16),
    }
    io = {"alog_pack": p.din("alog_pack", [128, 256], F32),
          "d_pack": p.din("d_pack", [128, 16], F32)}
    p.setup_consts()
    emit_B_common(p, io)
    emit_scan_pass(p, "f", aps)
    return p


def build_launch_B2(strict=False):
    p = Prog("scan_r_post", strict=strict)
    aps = {
        "delta": p.din("delta_r", [DI, T], F32),
        "u": p.din("u_r", [DI, T], BF16),
        "bc": p.din("bc_r", [32, T], F32),
        "z": p.din("z_bf", [DI, T], BF16),
        "wop": p.din("w_outproj", [DI, D], F32),
        "y_dst": p.dout("y_r", [D, T], BF16),
    }
    io = {
        "alog_pack": p.din("alog_pack", [128, 256], F32),
        "d_pack": p.din("d_pack", [128, 16], F32),
        "xT": p.din("xT", [D, T], F32),
        "y_f": p.din("y_f", [D, T], BF16),
        "w_dimred": p.din("w_dimred", [2 * D, D], F32),
        "w_glu": p.din("w_glu", [D, 2 * D], F32),
        "drb_pack": p.din("drb_pack", [128, 8], F32),
        "glb_pack": p.din("glb_pack", [128, 16], F32),
        "x1T": p.dout("x1T", [D, T], F32),
    }
    io["y_r"] = aps["y_dst"]
    p.setup_consts()
    emit_B_common(p, io)
    emit_scan_pass(p, "r", aps)
    p.seq.barrier()
    emit_B2_post(p, io)
    return p


def prep_B_weights(inp):
    w = {}
    w["w_outproj"] = np.ascontiguousarray(np.asarray(inp["L0_attn_out_proj_w"]).T)
    w["w_dimred"] = np.ascontiguousarray(np.asarray(inp["L0_attn_dim_reduce_w"]).T)
    w["w_glu"] = np.ascontiguousarray(np.asarray(inp["L0_attn_glu_w"]).T)
    al = np.asarray(inp["L0_attn_A_log"], np.float32)
    w["alog_pack"] = np.ascontiguousarray(
        al.reshape(16, 128, 16).transpose(1, 0, 2).reshape(128, 256))
    w["d_pack"] = _pack_cols(inp["L0_attn_D"], 16)
    w["drb_pack"] = _pack_cols(inp["L0_attn_dim_reduce_b"], 8)
    w["glb_pack"] = _pack_cols(inp["L0_attn_glu_b"], 16)
    return w


def emit_C(p, io):
    x1, gw, w1, w2, ws1, ws2 = (io["x1T"], io["w_gate"], io["w_exp1"],
                                io["w_exp2"], io["w_sh1"], io["w_sh2"])
    idn8 = p.sb("idn8", [8, 8], F32)
    idn128 = p.sb("idn128", [128, 128], F32)
    i8h = p.inline(np.eye(8, dtype=np.float32), "i8c")
    i128h = p.inline(np.eye(128, dtype=np.float32), "i128c")
    p.dma(idn8[:], i8h.ap())
    p.dma(idn128[:], i128h.ap())
    gw_sb = p.sb("gw_sb", [128, 8 * 8], F32)
    p.dma(rearr3(gw_sb, 8, 8), gw.rearrange("(c p) m -> p c m", p=128))
    wscr = io["w_scr"]

    x_sb = p.sb("c_x", [128, 8 * TC], F32)
    xb_sb = p.sb("c_xb", [128, 8 * TC], BF16)
    sc_sb = p.sb("c_sc", [8, TC], F32)
    tsc = p.sb("c_tsc", [128, 8 * 4], F32)
    m1n = p.sb("c_m1n", [128, 1], F32)
    m2n = p.sb("c_m2n", [128, 1], F32)
    rs = p.sb("c_rs", [128, 1], F32)
    iseq = p.sb("c_iseq", [128, 8], F32)
    sc2 = p.sb("c_sc2", [128, 8], F32)
    wfull = p.sb("c_wfull", [128, 8], F32)
    keep = p.sb("c_keep", [128, 8], F32)
    wtok = p.sb("c_wtok", [128, 8 * 4], F32)
    wT_sb = p.sb("c_wT", [8, TC], F32)
    wstrip = p.sb("c_wstrip", [1, 8 * TC], F32)
    ya = p.sb("c_ya", [128, 6 * TC], BF16)
    hg = p.sb("c_hg", [128, 6 * TC], BF16)
    zacc = p.sb("c_zacc", [128, 8 * TC], F32)
    ya2 = p.sb("c_ya2", [128, 12 * TC], BF16)
    hg2 = p.sb("c_hg2", [128, 12 * TC], BF16)
    p.sb("c_sg0", [128, TC], F32)
    p.sb("c_sg1", [128, TC], F32)
    p.sb_alias("lnsc_xb8", "c_ya2", [128, 8 * TC], BF16)
    p.sb_alias("lnsc_xq8", "c_hg2", [128, 8 * TC], BF16)
    p.sb_alias("lnsc_t", "c_sg0", [128, 512], F32)

    x1R = x1.rearrange("(c p) t -> p c t", p=128)
    for tc in range(NTC):
        tsl = slice(tc * TC, (tc + 1) * TC)
        p.dma(rearr3(x_sb, 8, TC), x1R[:, :, tsl])
        for c in range(8):
            p.vcopy(xb_sb[:, c * TC:(c + 1) * TC], x_sb[:, c * TC:(c + 1) * TC])
        gv = rearr3(gw_sb, 8, 8)
        for c in range(8):
            p.mm(p.pp[1][0:8, :], gv[:, c, :], x_sb[:, c * TC:(c + 1) * TC],
                 start=(c == 0), stop=(c == 7))
        p.acopy(sc_sb[:], p.pp[1][0:8, :])
        for q in range(4):
            p.trans(p.pp[2][:, 0:8], sc_sb[:, q * 128:(q + 1) * 128], idn8[:])
            p.acopy(tsc[:, q * 8:(q + 1) * 8], p.pp[2][:, 0:8])
        for q in range(4):
            s8 = tsc[:, q * 8:(q + 1) * 8]
            p.reduce(m1n[:], s8, ALU.max, negate=True)
            p.act(wfull[:], s8, AF.Exp, bias=m1n[:], accum_out=rs[:])
            p.recip(rs[:], rs[:])
            p.ts(wfull[:], wfull[:], rs[:], None, ALU.mult)
            p.ts(iseq[:], s8, m1n[:], 0.0, ALU.add, ALU.is_ge)
            p.stt(sc2[:], iseq[:], -1e30, s8, ALU.mult, ALU.add)
            p.reduce(m2n[:], sc2[:], ALU.max, negate=True)
            p.ts(keep[:], s8, m2n[:], 0.0, ALU.add, ALU.is_ge)
            p.tt(wtok[:, q * 8:(q + 1) * 8], wfull[:], keep[:], ALU.mult)
        for q in range(4):
            p.trans(p.pp[2][0:8, 0:128], wtok[:, q * 8:(q + 1) * 8], idn128[:])
            p.acopy(wT_sb[:, q * 128:(q + 1) * 128], p.pp[2][0:8, 0:128])
        p.dma(wscr[:, :], wT_sb[:])
        p.dma(wstrip[:].rearrange("p (r t) -> p r t", r=8), wscr[:, :].unsqueeze(0))
        for e in range(8):
            w1e = p.sb(f"c_w1e{e % 2}", [128, 8 * 1536], BF16)
            p.dma(rearr3(w1e, 8, 1536),
                  w1[e].rearrange("(c p) m -> p c m", p=128), engine="gpsimd")
            w2e = p.sb(f"c_w2e{e % 2}", [128, 6 * D], BF16)
            p.dma(rearr3(w2e, 6, D),
                  w2[e].rearrange("(c p) m -> p c m", p=128), engine="gpsimd")
            w1v = rearr3(w1e, 8, 1536)
            for k in range(6):
                for c in range(8):
                    p.mm(p.pp[k % 2][:], w1v[:, c, k * 128:(k + 1) * 128],
                         xb_sb[:, c * TC:(c + 1) * TC],
                         start=(c == 0), stop=(c == 7))
                p.acopy(ya[:, k * TC:(k + 1) * TC], p.pp[k % 2][:])
            p.mm(p.pp[3][:], p.ones_r_f32[:], wstrip[0:1, e * TC:(e + 1) * TC])
            for k in range(6):
                sg2 = p.sb(f"c_sg{k % 2}", [128, TC], F32)
                for c in range(8):
                    p.mm(p.pp[k % 2][:], w1v[:, c, 768 + k * 128:768 + (k + 1) * 128],
                         xb_sb[:, c * TC:(c + 1) * TC],
                         start=(c == 0), stop=(c == 7))
                p.act(sg2[:], p.pp[k % 2][:], AF.Silu)
                p.tt(sg2[:], sg2[:], p.pp[3][:], ALU.mult)
                p.tt(hg[:, k * TC:(k + 1) * TC], ya[:, k * TC:(k + 1) * TC],
                     sg2[:], ALU.mult)
            w2v = rearr3(w2e, 6, D)
            for oc in range(8):
                ev2 = p.sb(f"c_ev{oc % 2}", [128, TC], F32)
                for k in range(6):
                    p.mm(p.pp[oc % 2][:], w2v[:, k, oc * 128:(oc + 1) * 128],
                         hg[:, k * TC:(k + 1) * TC],
                         start=(k == 0), stop=(k == 5))
                if e == 0:
                    p.acopy(zacc[:, oc * TC:(oc + 1) * TC], p.pp[oc % 2][:])
                else:
                    p.acopy(ev2[:], p.pp[oc % 2][:])
                    p.tt(zacc[:, oc * TC:(oc + 1) * TC],
                         zacc[:, oc * TC:(oc + 1) * TC], ev2[:], ALU.add)
        for k in range(12):
            wt = stream_w_colblock(p, "c_ws1a", ws1, 8, k * 128, 128, par=k)
            wv = rearr3(wt, 8, 128)
            for c in range(8):
                p.mm(p.pp[k % 2][:], wv[:, c, :], xb_sb[:, c * TC:(c + 1) * TC],
                     start=(c == 0), stop=(c == 7))
            p.acopy(ya2[:, k * TC:(k + 1) * TC], p.pp[k % 2][:])
        for k in range(12):
            wt = stream_w_colblock(p, "c_ws1b", ws1, 8, 1536 + k * 128, 128, par=k)
            wv = rearr3(wt, 8, 128)
            sg2 = p.sb(f"c_sg{k % 2}", [128, TC], F32)
            for c in range(8):
                p.mm(p.pp[k % 2][:], wv[:, c, :], xb_sb[:, c * TC:(c + 1) * TC],
                     start=(c == 0), stop=(c == 7))
            p.act(sg2[:], p.pp[k % 2][:], AF.Silu)
            p.tt(hg2[:, k * TC:(k + 1) * TC], ya2[:, k * TC:(k + 1) * TC],
                 sg2[:], ALU.mult)
        for oc in range(8):
            wt = stream_w_colblock(p, "c_ws2", ws2, 12, oc * 128, 128, par=oc)
            wv = rearr3(wt, 12, 128)
            ev2 = p.sb(f"c_ev{oc % 2}", [128, TC], F32)
            for k in range(12):
                p.mm(p.pp[oc % 2][:], wv[:, k, :], hg2[:, k * TC:(k + 1) * TC],
                     start=(k == 0), stop=(k == 11))
            p.acopy(ev2[:], p.pp[oc % 2][:])
            p.tt(zacc[:, oc * TC:(oc + 1) * TC],
                 zacc[:, oc * TC:(oc + 1) * TC], ev2[:], ALU.add)
        for c in range(8):
            p.tt(zacc[:, c * TC:(c + 1) * TC], zacc[:, c * TC:(c + 1) * TC],
                 x_sb[:, c * TC:(c + 1) * TC], ALU.add)
        zch = [zacc[:, c * TC:(c + 1) * TC] for c in range(8)]
        o32 = [x_sb[:, c * TC:(c + 1) * TC] for c in range(8)]
        emit_ln(p, zch, None, TC, "cln", out_chunks_f32=o32)
        for c in range(8):
            p.dma(io["x2T"][c * 128:(c + 1) * 128, tsl],
                  x_sb[:, c * TC:(c + 1) * TC])


def build_launch_C(strict=False):
    p = Prog("moe", strict=strict)
    io = {
        "x1T": p.din("x1T", [D, T], F32),
        "w_gate": p.din("w_gate", [D, 8], F32),
        "w_exp1": p.din("w_exp1", [8, D, 1536], F32),
        "w_exp2": p.din("w_exp2", [8, 768, D], F32),
        "w_sh1": p.din("w_sh1", [D, 3072], F32),
        "w_sh2": p.din("w_sh2", [1536, D], F32),
        "x2T": p.dout("x2T", [D, T], F32),
    }
    p.setup_consts()
    io["w_scr"] = p.dscratch("w_scr", [8, TC], F32)
    emit_C(p, io)
    return p


def prep_C_weights(inp):
    w = {}
    w["w_gate"] = np.ascontiguousarray(np.asarray(inp["L0_ffn_gate_w"]).T)
    w["w_exp1"] = np.ascontiguousarray(
        np.asarray(inp["L0_ffn_W1"]).transpose(0, 2, 1))
    w["w_exp2"] = np.ascontiguousarray(
        np.asarray(inp["L0_ffn_W2"]).transpose(0, 2, 1))
    w["w_sh1"] = np.ascontiguousarray(np.asarray(inp["L0_ffn_Ws1"]).T)
    w["w_sh2"] = np.ascontiguousarray(np.asarray(inp["L0_ffn_Ws2"]).T)
    return w


def emit_D(p, io):
    x2, wqkv, wo = io["x2T"], io["w_qkvT"], io["w_attnoT"]
    qkv_sb = load_w_cast(p, "d_qkv", wqkv, 8, 3 * D)
    ow_sb = load_w_cast(p, "d_ow", wo, 8, D)
    bq_sb = p.sb("d_bq", [128, 16], F32); p.dma(bq_sb[:], io["qb_pack"])
    ob_sb = p.sb("d_ob", [128, 8], F32); p.dma(ob_sb[:], io["ob_pack"])
    brow_sb = p.sb("d_brow", [1, 3 * D], BF16)
    p.dma(brow_sb[:], io["qkvb_row"], engine="gpsimd")
    trimask = p.inline(np.triu(np.ones((128, 128), np.float32)), "triu128")
    tri_sb = p.sb("d_tri", [128, 128], F32)
    p.dma(tri_sb[:], trimask.ap())
    eps_c = p.const_col(1e-6)

    x_sb = p.sb("d_x", [128, 8 * TC], F32)
    xb_sb = p.sb("d_xb", [128, 8 * TC], BF16)
    qT = p.sb("d_qT", [128, 8 * TC], BF16)
    kT = p.sb("d_kT", [128, 8 * TC], BF16)
    ktok = p.sb("d_ktok", [128, 4 * D], BF16)
    vtok = p.sb("d_vtok", [128, 4 * D], BF16)
    attnT = p.sb("d_attnT", [128, 8 * TC], BF16)
    KVa = p.sb("d_KVa", [128, 8 * 65], F32)
    KVb = p.sb("d_KVb", [128, 8 * 65], BF16)
    x3_sb = p.sb("d_x3", [128, 8 * TC], F32)
    p.memset(KVa[:], 0.0)
    p.memset(KVb[:], 0.0)

    qv = rearr3(qkv_sb, 8, 3 * D)
    ov = rearr3(ow_sb, 8, D)
    x2R = x2.rearrange("(c p) t -> p c t", p=128)
    for tc in range(NTC):
        tsl = slice(tc * TC, (tc + 1) * TC)
        p.dma(rearr3(x_sb, 8, TC), x2R[:, :, tsl])
        for c in range(8):
            p.vcopy(xb_sb[:, c * TC:(c + 1) * TC], x_sb[:, c * TC:(c + 1) * TC])
        for which, dst in ((0, qT), (1, kT)):
            for jc in range(8):
                it = which * 8 + jc
                for c in range(8):
                    p.mm(p.pp[it % 2][:], qv[:, c, which * D + jc * 128:
                                        which * D + (jc + 1) * 128],
                         xb_sb[:, c * TC:(c + 1) * TC],
                         start=(c == 0), stop=(c == 7))
                dsl = dst[:, jc * TC:(jc + 1) * TC]
                p.ts(dsl, p.pp[it % 2][:], bq_sb[:, which * 8 + jc:which * 8 + jc + 1],
                     0.0, ALU.add, ALU.max)
                p.ts(dsl, dsl, eps_c, None, ALU.add)
        for sub in range(4):
            for which, dst in ((1, ktok), (2, vtok)):
                for eh in range(2):
                    it = sub * 4 + (which - 1) * 2 + eh
                    for c in range(8):
                        p.mm(p.pp[it % 2][:], xb_sb[:, c * TC + sub * 128:
                                               c * TC + sub * 128 + 128],
                             qv[:, c, which * D + eh * 512:which * D + (eh + 1) * 512],
                             start=(c == 0), stop=False)
                    p.mm(p.pp[it % 2][:], p.ones_r_bf[:],
                         brow_sb[0:1, which * D + eh * 512:which * D + (eh + 1) * 512],
                         start=False, stop=True)
                    dsl = dst[:, sub * D + eh * 512:sub * D + (eh + 1) * 512]
                    if which == 1:
                        p.ts(dsl, p.pp[it % 2][:], 0.0, eps_c, ALU.max, ALU.add)
                    else:
                        p.acopy(dsl, p.pp[it % 2][:])
        for sub in range(4):
            for h in range(16):
                hp = 64 * (h % 2)
                hc = h // 2
                Sm = p.sb(f"d_Sm{h % 2}", [128, 128], BF16)
                na = p.sb(f"d_na{h % 2}", [64, 128], F32)
                rd = p.sb(f"d_rd{h % 2}", [1, 128], F32)
                rd2 = p.sb(f"d_rd2{h % 2}", [1, 128], F32)
                qsl = qT[hp:hp + 64, hc * TC + sub * 128:hc * TC + sub * 128 + 128]
                ksl = kT[hp:hp + 64, hc * TC + sub * 128:hc * TC + sub * 128 + 128]
                ktk = ktok[:, sub * D + h * 64:sub * D + (h + 1) * 64]
                vtk = vtok[:, sub * D + h * 64:sub * D + (h + 1) * 64]
                p.mm(p.pp[0][:, 0:128], ksl, qsl)
                p.tt(Sm[:], p.pp[0][:, 0:128], tri_sb[:], ALU.mult)
                kva = KVa[hp:hp + 64, hc * 65:(hc + 1) * 65]
                kvb = KVb[hp:hp + 64, hc * 65:(hc + 1) * 65]
                kvb_kv = KVb[hp:hp + 64, hc * 65:hc * 65 + 64]
                kvb_kc = KVb[hp:hp + 64, hc * 65 + 64:hc * 65 + 65]
                p.mm(p.pp[1][0:64, 0:128], vtk, Sm[:], start=True, stop=False)
                p.mm(p.pp[1][0:64, 0:128], kvb_kv, qsl, start=False, stop=True)
                p.mm(p.pp[3][0:1, 0:128], p.ones_c_bf[:], Sm[:],
                     start=True, stop=False)
                p.mm(p.pp[3][0:1, 0:128], kvb_kc, qsl, start=False, stop=True)
                p.mm(p.pp[2][0:64, 0:64], ktk, vtk, start=True, stop=True)
                p.mm(p.pp[2][0:64, 64:65], ktk, p.ones_c_bf[:],
                     start=True, stop=True)
                p.ts(rd[:], p.pp[3][0:1, 0:128], eps_c[0:1, :], None, ALU.max)
                p.recip(rd2[:], rd[:])
                p.acopy(na[:], p.pp[1][0:64, 0:128])
                p.mm(p.pp[3][0:64, 0:128], p.ones_r_f32[0:1, 0:64], rd2[:],
                     start=True, stop=True)
                asl = attnT[hp:hp + 64, hc * TC + sub * 128:hc * TC + sub * 128 + 128]
                p.tt(asl, na[:], p.pp[3][0:64, 0:128], ALU.mult)
                p.tt(kva, kva, p.pp[2][0:64, 0:65], ALU.add)
                p.vcopy(kvb, kva)
        for oc in range(8):
            ev2 = p.sb(f"d_ev{oc % 2}", [128, TC], F32)
            for ec in range(8):
                p.mm(p.pp[oc % 2][:], ov[:, ec, oc * 128:(oc + 1) * 128],
                     attnT[:, ec * TC:(ec + 1) * TC],
                     start=(ec == 0), stop=(ec == 7))
            p.ts(ev2[:], p.pp[oc % 2][:], ob_sb[:, oc:oc + 1], None, ALU.add)
            p.tt(x3_sb[:, oc * TC:(oc + 1) * TC], ev2[:],
                 x_sb[:, oc * TC:(oc + 1) * TC], ALU.add)
        xch = [x3_sb[:, c * TC:(c + 1) * TC] for c in range(8)]
        emit_ln(p, xch, None, TC, "dln", out_chunks_f32=xch)
        for c in range(8):
            p.dma(io["x3T"][c * 128:(c + 1) * 128, tsl],
                  x3_sb[:, c * TC:(c + 1) * TC])


def build_launch_D(strict=False):
    p = Prog("linattn", strict=strict)
    io = {
        "x2T": p.din("x2T", [D, T], F32),
        "w_qkvT": p.din("w_qkvT", [D, 3 * D], F32),
        "qb_pack": p.din("qb_pack", [128, 16], F32),
        "qkvb_row": p.din("qkvb_row", [1, 3 * D], F32),
        "w_attnoT": p.din("w_attnoT", [D, D], F32),
        "ob_pack": p.din("ob_pack", [128, 8], F32),
        "x3T": p.dout("x3T", [D, T], F32),
    }
    p.setup_consts()
    emit_D(p, io)
    return p


def prep_D_weights(inp):
    w = {}
    w["w_qkvT"] = np.ascontiguousarray(np.asarray(inp["L1_attn_qkv_w"]).T)
    qb = np.asarray(inp["L1_attn_qkv_b"], np.float32)
    w["qb_pack"] = np.concatenate(
        [_pack_cols(qb[0:D], 8), _pack_cols(qb[D:2 * D], 8)], axis=1)
    w["qkvb_row"] = np.ascontiguousarray(qb.reshape(1, 3 * D))
    w["w_attnoT"] = np.ascontiguousarray(np.asarray(inp["L1_attn_out_w"]).T)
    w["ob_pack"] = _pack_cols(inp["L1_attn_out_b"], 8)
    return w


DIM_INNER = 2730
NHK1 = 22


def emit_E(p, io):
    xs, wq, wo, w1, w2 = io["xsT"], io["w_lqkv"], io["w_lo"], io["w_l1"], io["w_l2"]
    m0 = np.triu(np.ones((128, 128), np.float32))
    m2 = np.tril(np.ones((128, 128), np.float32))
    mb0 = ((1 - m0) * -1e30).astype(np.float32)
    mb1 = np.zeros((128, 128), np.float32)
    mb2 = ((1 - m2) * -1e30).astype(np.float32)
    mbias_sb = p.sb("e_mbias", [128, 384], F32)
    p.dma(mbias_sb[:], p.inline(np.concatenate([mb0, mb1, mb2], 1), "e_mbias_c").ap())
    m01T = np.concatenate([m0.T, np.ones((128, 128), np.float32), m2.T], 1)
    m01T_sb = p.sb("e_m01T", [128, 384], F16)
    p.dma(m01T_sb[:], p.inline(m01T.astype(np.float16), "e_m01T_c").ap(), engine="gpsimd")
    rotm = np.zeros((128, 128), np.float32)
    for base in (0, 64):
        for i in range(32):
            rotm[base + i, base + 32 + i] = -1.0
            rotm[base + 32 + i, base + i] = 1.0
    rotT_sb = p.sb("e_rotT", [128, 128], F16)
    p.dma(rotT_sb[:], p.inline(rotm.T.astype(np.float16), "e_rotT_c").ap(), engine="gpsimd")
    ones2 = np.zeros((128, 2), np.float32)
    ones2[0:64, 0] = 1.0; ones2[64:128, 1] = 1.0
    ones2_sb = p.sb("e_ones2", [128, 2], F16)
    p.dma(ones2_sb[:], p.inline(ones2.astype(np.float16), "e_ones2_c").ap(), engine="gpsimd")
    sel2 = np.zeros((2, 128), np.float32)
    sel2[0, 0:64] = 1.0; sel2[1, 64:128] = 1.0
    sel2_sb = p.sb("e_sel2", [2, 128], F16)
    p.dma(sel2_sb[:], p.inline(sel2.astype(np.float16), "e_sel2_c").ap(), engine="gpsimd")
    idn16 = p.sb("idn_f16", [128, 128], F16)
    p.dma(idn16[:], p.inline(np.eye(128, dtype=np.float16), "idn_f16_c").ap(),
          engine="gpsimd")

    wv_sb = p.sb("e_wv", [128, 8 * D], BF16)
    p.dma(rearr3(wv_sb, 8, D),
          wq.rearrange("(c p) m -> p c m", p=128)[:, :, 2 * D:3 * D],
          engine="gpsimd")
    lng_sb = p.sb("e_lng", [128, 8], F32); p.dma(lng_sb[:], io["lng_pack"])
    lnb_sb = p.sb("e_lnb", [128, 8], F32); p.dma(lnb_sb[:], io["lnb_pack"])
    b1a_sb = p.sb("e_b1a", [128, NHK1], F32); p.dma(b1a_sb[:], io["b1a_pack"])
    b1g_sb = p.sb("e_b1g", [128, NHK1], F32); p.dma(b1g_sb[:], io["b1g_pack"])
    b2_sb = p.sb("e_b2p", [128, 8], F32); p.dma(b2_sb[:], io["b2_pack"])
    qs_sb = p.sb("e_qsp", [128, 1], F32); p.dma(qs_sb[:], io["qs_pack"])
    ks_sb = p.sb("e_ksp", [128, 1], F32); p.dma(ks_sb[:], io["ks_pack"])

    halo = p.sb("e_halo", [128, 8 * 513], F32)
    xq_bf = p.sb("e_xq", [128, 8 * TC], F16)
    qrot = p.sb("e_qrot", [128, 8 * TC], F16)
    krot = p.sb("e_krot", [128, 8 * TC], F16)
    kprev = p.sb("e_kprev", [128, 8 * 256], F16)
    vtok = p.sb("e_vtok", [128, 4 * D], F16)
    vprev = p.sb("e_vprev", [128, 2 * D], F16)
    attnT = p.sb("e_attnT", [128, 8 * TC], F16)
    hg = p.sb("e_hg", [128, NHK1 * TC], F16)

    p.sb_alias("lnsc_xb8", "e_attnT", [128, 8 * TC], F16)
    p.sb_alias("lnsc_xq8", "e_qrot", [128, 8 * TC], F16)

    xsR = xs.rearrange("(c p) t -> p c t", p=128)
    wvv = rearr3(wv_sb, 8, D)

    def load_shifted(src_hbm_R, tc):
        for c in range(8):
            hsl = halo[:, c * 513:(c + 1) * 513]
            if tc == 0:
                p.memset(hsl[:, 0:1], 0.0)
                p.dma(hsl[:, 1:513], src_hbm_R[:, c, 0:TC])
            else:
                p.dma(hsl, src_hbm_R[:, c, tc * TC - 1:(tc + 1) * TC])
            off = 1 if c < 4 else 0
            p.vcopy(xq_bf[:, c * TC:(c + 1) * TC], hsl[:, off:off + TC])

    for tc in range(NTC):
        tsl = slice(tc * TC, (tc + 1) * TC)
        tp = tc % 2
        cq = p.sb(f"e_cq{tp}", [128, TC], F16)
        sq_ = p.sb(f"e_sq{tp}", [128, TC], F16)
        ck = p.sb(f"e_ck{tp}", [128, TC], F16)
        sk_ = p.sb(f"e_sk{tp}", [128, TC], F16)
        for src, dstt in [(io["t_csq"], cq), (io["t_ssq"], sq_),
                          (io["t_csk"], ck), (io["t_ssk"], sk_)]:
            p.dma(dstt[:], src[:, tsl], engine="gpsimd")
        load_shifted(xsR, tc)
        for which, dst, ct, st_, scp in ((0, qrot, cq, sq_, qs_sb),
                                         (1, krot, ck, sk_, ks_sb)):
            for jc in range(8):
                it = which * 8 + jc
                wt = stream_w_colblock(p, "e_wqk", wq, 8,
                                       which * D + jc * 128, 128, par=it)
                wqk = rearr3(wt, 8, 128)
                qraw = p.sb(f"e_qraw{it % 2}", [128, TC], F16)
                sqb = p.sb(f"e_sqb{it % 2}", [128, TC], F16)
                nsb = p.sb(f"e_nsb{it % 2}", [2, TC], F32)
                rnorm = p.sb(f"e_rnorm{it % 2}", [2, TC], F16)
                t1b = p.sb(f"e_t1b{it % 2}", [128, TC], F16)
                t2b = p.sb(f"e_t2b{it % 2}", [128, TC], F16)
                for c in range(8):
                    p.mm(p.pp[it % 2][:], wqk[:, c, :],
                         xq_bf[:, c * TC:(c + 1) * TC],
                         start=(c == 0), stop=(c == 7))
                p.acopy(qraw[:], p.pp[it % 2][:])
                p.act(sqb[:], p.pp[it % 2][:], AF.Square)
                p.mm(p.pq[:], ones2_sb[:], sqb[:], start=True, stop=True)
                p.act(nsb[:], p.pq[:], AF.Sqrt)
                p.ts(nsb[:], nsb[:], p.const_col(1e-12)[0:2, :], None, ALU.max)
                p.recip(nsb[:], nsb[:])
                p.vcopy(rnorm[:], nsb[:])
                p.mm(p.pp[2 + it % 2][:], sel2_sb[:], rnorm[:], start=True, stop=True)
                p.tt(t1b[:], qraw[:], p.pp[2 + it % 2][:], ALU.mult)
                p.ts(t1b[:], t1b[:], scp[:], None, ALU.mult)
                p.mm(p.pp[2 + it % 2][:, 0:TC], rotT_sb[:], t1b[:],
                     start=True, stop=True)
                rsl = dst[:, jc * TC:(jc + 1) * TC]
                p.tt(t2b[:], t1b[:], ct[:], ALU.mult)
                p.tt(qraw[:], p.pp[2 + it % 2][:, 0:TC], st_[:], ALU.mult)
                p.tt(rsl, t2b[:], qraw[:], ALU.add)
        for sub in range(4):
            for eh in range(2):
                it = sub * 2 + eh
                for c in range(8):
                    p.mm(p.pp[it % 2][:], xq_bf[:, c * TC + sub * 128:
                                           c * TC + sub * 128 + 128],
                         wvv[:, c, eh * 512:(eh + 1) * 512],
                         start=(c == 0), stop=(c == 7))
                p.acopy(vtok[:, sub * D + eh * 512:sub * D + (eh + 1) * 512],
                        p.pp[it % 2][:])
        for sub in range(4):
            qc = tc * 4 + sub
            for h in range(16):
                hp = 64 * (h % 2)
                hc = h // 2
                Sb = p.sb(f"e_Sbuf{h % 2}", [128, 384], F32)
                m8n = p.sb(f"e_m8n{h % 2}", [128, 1], F32)
                m8b = p.sb(f"e_m8b{h % 2}", [128, 1], F16)
                mrow = p.sb(f"e_mrow{h % 2}", [1, 128], F16)
                PT = p.sb(f"e_PT{h % 2}", [128, 384], F16)
                na = p.sb(f"e_na{h % 2}", [64, 128], F32)
                rd2 = p.sb(f"e_rd2{h % 2}", [1, 128], F32)
                qsl = qrot[hp:hp + 64, hc * TC + sub * 128:hc * TC + sub * 128 + 128]

                def kblk(kb):
                    if kb >= tc * 4:
                        s = kb - tc * 4
                        return (krot[hp:hp + 64, hc * TC + s * 128:
                                     hc * TC + s * 128 + 128],
                                vtok[:, s * D + h * 64:s * D + (h + 1) * 64])
                    s = kb - (tc * 4 - 2)
                    return (kprev[hp:hp + 64, hc * 256 + s * 128:
                                  hc * 256 + s * 128 + 128],
                            vprev[:, s * D + h * 64:s * D + (h + 1) * 64])

                blocks = [b for b in (qc - 2, qc - 1, qc) if b >= 0]
                nb = len(blocks)
                boff = 3 - nb
                for bi, kb in enumerate(blocks):
                    ka, _ = kblk(kb)
                    p.mm(p.pp[bi][:, 0:128], qsl, ka, start=True, stop=True)
                    p.tt(Sb[:, (boff + bi) * 128:(boff + bi + 1) * 128],
                         p.pp[bi][:, 0:128],
                         mbias_sb[:, (boff + bi) * 128:(boff + bi + 1) * 128],
                         ALU.add)
                p.reduce(m8n[:], Sb[:, boff * 128:384], ALU.max, negate=True)
                p.vcopy(m8b[:], m8n[:])
                p.mm(p.pp[3][0:1, 0:128], m8b[:], idn16[:], start=True, stop=True)
                p.acopy(mrow[:], p.pp[3][0:1, 0:128])
                for bi, kb in enumerate(blocks):
                    ka, _ = kblk(kb)
                    p.mm(p.pp[bi][:, 0:128], ka, qsl, start=True, stop=False)
                    p.mm(p.pp[bi][:, 0:128], p.ones_r_f16[:], mrow[:],
                         start=False, stop=True)
                    p.act(PT[:, (boff + bi) * 128:(boff + bi + 1) * 128],
                          p.pp[bi][:, 0:128], AF.Exp, scale=8.0)
                p.tt(PT[:, boff * 128:384], PT[:, boff * 128:384],
                     m01T_sb[:, boff * 128:384], ALU.mult)
                for bi, kb in enumerate(blocks):
                    _, va = kblk(kb)
                    p.mm(p.pp[3][0:64, 0:128], va,
                         PT[:, (boff + bi) * 128:(boff + bi + 1) * 128],
                         start=(bi == 0), stop=(bi == nb - 1))
                for bi in range(nb):
                    p.mm(p.pr[0][0:1, 0:128], p.ones_c_f16[:],
                         PT[:, (boff + bi) * 128:(boff + bi + 1) * 128],
                         start=(bi == 0), stop=(bi == nb - 1))
                p.recip(rd2[:], p.pr[0][0:1, 0:128])
                p.acopy(na[:], p.pp[3][0:64, 0:128])
                p.mm(p.pp[3][0:64, 0:128], p.ones_r_f32[0:1, 0:64], rd2[:],
                     start=True, stop=True)
                asl = attnT[hp:hp + 64, hc * TC + sub * 128:hc * TC + sub * 128 + 128]
                p.tt(asl, na[:], p.pp[3][0:64, 0:128], ALU.mult)
        for c in range(8):
            p.vcopy(kprev[:, c * 256:(c + 1) * 256],
                    krot[:, c * TC + 256:(c + 1) * TC])
        p.vcopy(vprev[:, 0:D], vtok[:, 2 * D:3 * D])
        p.vcopy(vprev[:, D:2 * D], vtok[:, 3 * D:4 * D])
        for oc in range(8):
            wt = stream_w_colblock(p, "e_wob", wo, 8, oc * 128, 128, par=oc)
            wov = rearr3(wt, 8, 128)
            ev = p.sb(f"e_ev{oc % 2}", [128, TC], F32)
            xtmp = p.sb(f"e_xtmp{oc % 2}", [128, TC], F32)
            for ec in range(8):
                p.mm(p.pp[oc % 2][:], wov[:, ec, :],
                     attnT[:, ec * TC:(ec + 1) * TC],
                     start=(ec == 0), stop=(ec == 7))
            p.acopy(ev[:], p.pp[oc % 2][:])
            p.tt(xtmp[:], ev[:], halo[:, oc * 513 + 1:oc * 513 + 513], ALU.add)
            p.dma(io["xmidT"][oc * 128:(oc + 1) * 128, tsl], xtmp[:])

    xmR = io["xmidT"].rearrange("(c p) t -> p c t", p=128)
    for tc in range(NTC):
        tsl = slice(tc * TC, (tc + 1) * TC)
        load_shifted(xmR, tc)
        shf = [halo[:, c * 513 + (1 if c < 4 else 0):
                    c * 513 + (1 if c < 4 else 0) + TC] for c in range(8)]
        och = [xq_bf[:, c * TC:(c + 1) * TC] for c in range(8)]
        emit_ln(p, shf, och, TC, "eln", dt16=F16,
                g_ap=[lng_sb[:, c:c + 1] for c in range(8)],
                b_ap=[lnb_sb[:, c:c + 1] for c in range(8)])
        for k in range(NHK1):
            wcols = 128 if k < NHK1 - 1 else DIM_INNER - 128 * (NHK1 - 1)
            aa = p.sb(f"e_aa{k % 2}", [128, TC], F16)
            gg = p.sb(f"e_gg{k % 2}", [128, TC], F32)
            wt = stream_w_colblock(p, "e_w1a", w1, 8, k * 128, wcols,
                                   bufcols=128, par=k)
            wv = rearr3(wt, 8, 128)
            for c in range(8):
                p.mm(p.pp[k % 2][0:wcols, :], wv[:, c, 0:wcols],
                     xq_bf[:, c * TC:(c + 1) * TC],
                     start=(c == 0), stop=(c == 7))
            p.ts(aa[0:wcols, :], p.pp[k % 2][0:wcols, :],
                 b1a_sb[0:wcols, k:k + 1], None, ALU.add)
            wt2 = stream_w_colblock(p, "e_w1g", w1, 8, DIM_INNER + k * 128,
                                    wcols, bufcols=128, par=k + 1)
            wv2 = rearr3(wt2, 8, 128)
            for c in range(8):
                p.mm(p.pp[2 + k % 2][0:wcols, :], wv2[:, c, 0:wcols],
                     xq_bf[:, c * TC:(c + 1) * TC],
                     start=(c == 0), stop=(c == 7))
            p.act(gg[0:wcols, :], p.pp[2 + k % 2][0:wcols, :], AF.Gelu,
                  bias=b1g_sb[0:wcols, k:k + 1])
            p.tt(hg[0:wcols, k * TC:(k + 1) * TC], aa[0:wcols, :],
                 gg[0:wcols, :], ALU.mult)
        w2R = w2.rearrange("(c p) m -> p c m", p=128)
        for oc in range(8):
            wt = p.sb(f"e_w2b{oc % 2}", [128, NHK1 * 128], F16)
            p.dma(rearr3(wt, NHK1, 128), w2R[:, :, oc * 128:(oc + 1) * 128],
                  engine="gpsimd")
            wv = rearr3(wt, NHK1, 128)
            ev = p.sb(f"e_ev{oc % 2}", [128, TC], F32)
            xtmp = p.sb(f"e_xtmp{oc % 2}", [128, TC], F32)
            for k in range(NHK1):
                krows = 128 if k < NHK1 - 1 else DIM_INNER - 128 * (NHK1 - 1)
                p.mm(p.pp[oc % 2][:], wv[0:krows, k, :],
                     hg[0:krows, k * TC:(k + 1) * TC],
                     start=(k == 0), stop=(k == NHK1 - 1))
            p.ts(ev[:], p.pp[oc % 2][:], b2_sb[:, oc:oc + 1], None, ALU.add)
            p.tt(xtmp[:], ev[:], halo[:, oc * 513 + 1:oc * 513 + 513], ALU.add)
            p.dma(io["xoT"][oc * 128:(oc + 1) * 128, tsl], xtmp[:])


def declare_E_io(p, suffix=""):
    return {
        "w_lqkv": p.din(f"w_lqkv{suffix}", [D, 3 * D], F32),
        "w_lo": p.din(f"w_lo{suffix}", [D, D], F32),
        "w_l1": p.din(f"w_l1{suffix}", [D, 2 * DIM_INNER], F32),
        "w_l2": p.din(f"w_l2{suffix}", [NHK1 * 128, D], F32),
        "qs_pack": p.din(f"qs_pack{suffix}", [128, 1], F32),
        "ks_pack": p.din(f"ks_pack{suffix}", [128, 1], F32),
        "lng_pack": p.din(f"lng_pack{suffix}", [128, 8], F32),
        "lnb_pack": p.din(f"lnb_pack{suffix}", [128, 8], F32),
        "b1a_pack": p.din(f"b1a_pack{suffix}", [128, NHK1], F32),
        "b1g_pack": p.din(f"b1g_pack{suffix}", [128, NHK1], F32),
        "b2_pack": p.din(f"b2_pack{suffix}", [128, 8], F32),
    }


def declare_E_tables(p):
    return {
        "t_csq": p.din("t_csq", [128, T], F32),
        "t_ssq": p.din("t_ssq", [128, T], F32),
        "t_csk": p.din("t_csk", [128, T], F32),
        "t_ssk": p.din("t_ssk", [128, T], F32),
    }


def build_launch_E(strict=False):
    p = Prog("l1ffn_layer", strict=strict)
    io = declare_E_io(p) | declare_E_tables(p)
    io["xsT"] = p.din("xsT", [D, T], F32)
    io["xmidT"] = p.dout("xmidT", [D, T], F32)
    io["xoT"] = p.dout("xoT", [D, T], F32)
    p.setup_consts()
    emit_E(p, io)
    return p


def rotary_tables_host():
    DIM_HEAD, WINDOW, n = 64, 256, T
    half = np.arange(0, DIM_HEAD, 2, dtype=np.float32)
    inv_freq = 1.0 / 10000.0 ** (half / DIM_HEAD)
    t = np.arange(n, dtype=np.float32)
    freqs = t[:, None] * inv_freq[None, :]
    freqs = np.concatenate([freqs, freqs], -1)
    base_scale = (half + 0.4 * DIM_HEAD) / (1.4 * DIM_HEAD)
    power = (t - n // 2) / (WINDOW // 2)
    scale = base_scale[None, :] ** power[:, None]
    scale = np.concatenate([scale, scale], -1)
    cos, sin = np.cos(freqs), np.sin(freqs)
    csq = (cos * scale).T; ssq = (sin * scale).T
    csk = (cos / scale).T; ssk = (sin / scale).T
    dbl = lambda a: np.ascontiguousarray(
        np.concatenate([a, a], 0).astype(np.float32))
    return dbl(csq), dbl(ssq), dbl(csk), dbl(ssk)


def prep_E_weights(inp, i):
    w = {}
    w["w_lqkv"] = np.ascontiguousarray(np.asarray(inp["L1_ffn_Wqkv"])[i])
    w["w_lo"] = np.ascontiguousarray(np.asarray(inp["L1_ffn_Wo"])[i])
    w["w_l1"] = np.ascontiguousarray(np.asarray(inp["L1_ffn_W1"])[i])
    w2 = np.asarray(inp["L1_ffn_W2"])[i]
    w2p = np.zeros((NHK1 * 128, D), np.float32)
    w2p[0:DIM_INNER] = w2
    w["w_l2"] = w2p
    qs = np.asarray(inp["L1_ffn_q_scale"])[i]
    ks = np.asarray(inp["L1_ffn_k_scale"])[i]
    w["qs_pack"] = np.ascontiguousarray(np.tile(qs, 2).reshape(128, 1))
    w["ks_pack"] = np.ascontiguousarray(np.tile(ks, 2).reshape(128, 1))
    w["lng_pack"] = _pack_cols(np.asarray(inp["L1_ffn_ln_g"])[i], 8)
    w["lnb_pack"] = _pack_cols(np.asarray(inp["L1_ffn_ln_b"])[i], 8)
    b1 = np.asarray(inp["L1_ffn_b1"])[i]
    pad = np.zeros(NHK1 * 128, np.float32)
    pad[0:DIM_INNER] = b1[0:DIM_INNER]
    w["b1a_pack"] = _pack_cols(pad, NHK1)
    pad2 = np.zeros(NHK1 * 128, np.float32)
    pad2[0:DIM_INNER] = b1[DIM_INNER:]
    w["b1g_pack"] = _pack_cols(pad2, NHK1)
    w["b2_pack"] = _pack_cols(np.asarray(inp["L1_ffn_b2"])[i], 8)
    return w


def prep_E_tables():
    csq, ssq, csk, ssk = rotary_tables_host()
    return {"t_csq": csq, "t_ssq": ssq, "t_csk": csk, "t_ssk": ssk}


def _pack_cols(vec, nchunks):
    v = np.asarray(vec, np.float32).reshape(nchunks, 128).T
    return np.ascontiguousarray(v)


def prep_A_weights(inp):
    w = {}
    w["w_inproj"] = np.ascontiguousarray(np.asarray(inp["L0_attn_in_proj_w"]).T)
    w["w_xproj"] = np.ascontiguousarray(np.asarray(inp["L0_attn_x_proj_w"]).T)
    w["w_dtproj"] = np.ascontiguousarray(np.asarray(inp["L0_attn_dt_proj_w"]).T)
    cwf = np.asarray(inp["L0_attn_conv_w"], np.float32).reshape(DI, 4)
    w["convw_pack"] = np.ascontiguousarray(
        cwf.reshape(16, 128, 4).transpose(1, 0, 2).reshape(128, 64))
    w["convb_pack"] = _pack_cols(inp["L0_attn_conv_b"], 16)
    w["dtb_pack"] = _pack_cols(inp["L0_attn_dt_proj_b"], 16)
    w["lng_pack"] = _pack_cols(inp["L0_attn_ln_g"], 8)
    w["lnb_pack"] = _pack_cols(inp["L0_attn_ln_b"], 8)
    return w


def emit_F(p, io):
    xsR = io["xsT"].rearrange("(c p) t -> p c t", p=128)
    x3R = io["x3T"].rearrange("(c p) t -> p c t", p=128)
    for tc in range(NTC):
        tsl = slice(tc * TC, (tc + 1) * TC)
        a_sb = p.sb(f"f_a{tc % 2}", [128, 8 * TC], F32)
        b_sb = p.sb(f"f_b{tc % 2}", [128, 8 * TC], F32)
        p.dma(rearr3(a_sb, 8, TC), xsR[:, :, tsl])
        p.dma(rearr3(b_sb, 8, TC), x3R[:, :, tsl])
        for c in range(8):
            p.tt(a_sb[:, c * TC:(c + 1) * TC], a_sb[:, c * TC:(c + 1) * TC],
                 b_sb[:, c * TC:(c + 1) * TC], ALU.add)
        xch = [a_sb[:, c * TC:(c + 1) * TC] for c in range(8)]
        emit_ln(p, xch, None, TC, "fln", out_chunks_f32=xch)
        for c in range(8):
            p.dma(io["x4T"][c * 128:(c + 1) * 128, tsl],
                  a_sb[:, c * TC:(c + 1) * TC])


def build_launch_F(strict=False):
    p = Prog("final_ln", strict=strict)
    io = {"xsT": p.din("xsT", [D, T], F32), "x3T": p.din("x3T", [D, T], F32),
          "x4T": p.dout("x4T", [D, T], F32)}
    p.setup_consts()
    emit_F(p, io)
    return p


def build_fused(strict=False):
    p = Prog("megakernel", strict=strict)
    ioA = declare_A_io(p, ext=False)
    ioB = {
        "alog_pack": p.din("alog_pack", [128, 256], F32),
        "d_pack": p.din("d_pack", [128, 16], F32),
        "w_outproj": p.din("w_outproj", [DI, D], F32),
        "w_dimred": p.din("w_dimred", [2 * D, D], F32),
        "w_glu": p.din("w_glu", [D, 2 * D], F32),
        "drb_pack": p.din("drb_pack", [128, 8], F32),
        "glb_pack": p.din("glb_pack", [128, 16], F32),
        "xT": ioA["xT"],
        "y_f": p.dscratch("y_f", [D, T], BF16),
        "y_r": p.dscratch("y_r", [D, T], BF16),
        "x1T": p.dscratch("x1T", [D, T], F32),
    }
    ioC = {
        "x1T": ioB["x1T"],
        "w_gate": p.din("w_gate", [D, 8], F32),
        "w_exp1": p.din("w_exp1", [8, D, 1536], F32),
        "w_exp2": p.din("w_exp2", [8, 768, D], F32),
        "w_sh1": p.din("w_sh1", [D, 3072], F32),
        "w_sh2": p.din("w_sh2", [1536, D], F32),
        "w_scr": p.dscratch("w_scr", [8, TC], F32),
        "x2T": p.dscratch("x2T", [D, T], F32),
    }
    ioD = {
        "x2T": ioC["x2T"],
        "w_qkvT": p.din("w_qkvT", [D, 3 * D], F32),
        "qb_pack": p.din("qb_pack", [128, 16], F32),
        "qkvb_row": p.din("qkvb_row", [1, 3 * D], F32),
        "w_attnoT": p.din("w_attnoT", [D, D], F32),
        "ob_pack": p.din("ob_pack", [128, 8], F32),
        "x3T": p.dscratch("x3T", [D, T], F32),
    }
    tabs = declare_E_tables(p)
    ioEs = []
    xmid = p.dscratch("xmidT", [D, T], F32)
    xo_a = p.dscratch("xo_a", [D, T], F32)
    xo_b = p.dscratch("xo_b", [D, T], F32)
    for i in range(6):
        io = declare_E_io(p, suffix=f"_{i}") | tabs
        io["xsT"] = ioD["x3T"] if i == 0 else (xo_a if i % 2 == 1 else xo_b)
        io["xmidT"] = xmid
        io["xoT"] = xo_a if i % 2 == 0 else xo_b
        ioEs.append(io)
    ioF = {"xsT": ioEs[5]["xoT"], "x3T": ioD["x3T"],
           "x4T": p.dout("x4T", [D, T], F32)}

    p.setup_consts()
    p.make_arena()

    p.begin_stage("A")
    emit_A(p, ioA)
    p.begin_stage("B1")
    emit_B_common(p, ioB)
    emit_scan_pass(p, "f", {"delta": ioA["delta_f"], "u": ioA["u_f"],
                            "bc": ioA["bc_f"], "z": ioA["z_bf"],
                            "wop": ioB["w_outproj"], "y_dst": ioB["y_f"]})
    p.begin_stage("B2")
    emit_B_common(p, ioB)
    emit_scan_pass(p, "r", {"delta": ioA["delta_r"], "u": ioA["u_r"],
                            "bc": ioA["bc_r"], "z": ioA["z_bf"],
                            "wop": ioB["w_outproj"], "y_dst": ioB["y_r"]})
    p.begin_stage("B2p")
    emit_B2_post(p, ioB)
    p.begin_stage("C")
    emit_C(p, ioC)
    p.begin_stage("D")
    emit_D(p, ioD)
    for i in range(6):
        p.begin_stage(f"E{i}")
        emit_E(p, ioEs[i])
    p.begin_stage("F")
    emit_F(p, ioF)
    return p


def kernel(**inputs):
    inputs = {k: np.asarray(v) for k, v in inputs.items()}
    NB = inputs["L0_attn_x"].shape[0]
    xTs = [np.ascontiguousarray(inputs["L0_attn_x"][b].T) for b in range(NB)]

    base = {}
    base.update(prep_A_weights(inputs))
    base.update(prep_B_weights(inputs))
    base.update(prep_C_weights(inputs))
    base.update(prep_D_weights(inputs))
    base.update(prep_E_tables())
    for i in range(6):
        for k, v in prep_E_weights(inputs, i).items():
            base[f"{k}_{i}"] = v

    p = build_fused()
    in_maps = [dict(base, xT=xTs[b]) for b in range(NB)]
    res = p.run(in_maps)
    out = np.stack([np.asarray(res.results[b]["x4T"]).T for b in range(NB)],
                   axis=0)
    return np.ascontiguousarray(out, dtype=np.float32)
